# version 01 (final)
# baseline (speedup 1.0000x reference)
import numpy as np
from concourse.bacc import Bacc
import concourse.mybir as mybir

F32 = mybir.dt.float32
BF16 = mybir.dt.bfloat16
I16 = mybir.dt.int16
I32 = mybir.dt.int32
U32 = mybir.dt.uint32
ALU = mybir.AluOpType
AF = mybir.ActivationFunctionType
AX = mybir.AxisListType

T = 2048
D = 1024
NG = T // 128
NCH = D // 128
NH = 16
DH = 64
CAP = 5120
NE = 8
EPS = 1e-5


class Sem:
    def __init__(self, nc, name):
        self.h = nc.alloc_semaphore(name)
        self.n = 0

    def inc(self, ins, k=1):
        ins.then_inc(self.h, k)
        self.n += k
        return self.n

    def dma(self, ins):
        return self.inc(ins, 16)


class Stream:

    def __init__(self, nc, name, n=2):
        self.sems = [Sem(nc, f"{name}{i}") for i in range(n)]
        self.n = n

    def issue(self, ins, par):
        s = self.sems[par % self.n]
        s.dma(ins)
        return (s.h, s.n)

    def wait(self, eng, tok):
        eng.wait_ge(tok[0], tok[1])

    def predict(self, pars):
        cnt = {i: s.n for i, s in enumerate(self.sems)}
        toks = []
        for p in pars:
            i = p % self.n
            cnt[i] += 16
            toks.append((self.sems[i].h, cnt[i]))
        return toks


class MK:
    def __init__(self, upto=1, debug=False):
        self.upto = upto
        self.debug = debug
        nc = self.nc = Bacc("TRN2", target_bir_lowering=False)
        dt_ = nc.dram_tensor
        self.x_in = dt_("x", [T, D], F32, kind="ExternalInput").ap()
        self.wqkv = dt_("wqkv", [3, D, D], F32, kind="ExternalInput").ap()
        self.out_x = dt_("out_x", [T, D], F32, kind="ExternalOutput").ap()
        self.x_dram = dt_("x_dram", [T, D], F32, kind="Internal").ap()
        self.stg_tm = dt_("stg_tm", [T, D], BF16, kind="Internal").ap()
        self.stg_fm = dt_("stg_fm", [1152, T], BF16, kind="Internal").ap()
        if self.upto >= 2:
            self.g_gamma = dt_("lngamma", [1, D], F32, kind="ExternalInput").ap()
            self.g_beta = dt_("lnbeta", [1, D], F32, kind="ExternalInput").ap()
            self.rwT = dt_("rwT", [D, 8], F32, kind="ExternalInput").ap()
            self.w12T = dt_("w12T", [D, 8192], F32, kind="ExternalInput").ap()
            self.w3T = dt_("w3T", [4096, D], F32, kind="ExternalInput").ap()
            self.gidp1_in = dt_("gidp1", [128, NG], F32, kind="ExternalInput").ap()
            self.sel_in = dt_("selm", [64, 8], F32, kind="ExternalInput").ap()
            self.vrows_in = dt_("vrows", [8, 1], I32, kind="ExternalInput").ap()
            self.selr_in = dt_("selr", [64, 8], F32, kind="ExternalInput").ap()
            self.c_pool_in = dt_("c_pool_in", [T, D], BF16, kind="Internal").ap()
            self.c_pool_out = dt_("c_pool_out", [8 * T, D], BF16, kind="Internal").ap()
            self.c_cnt_in = dt_("c_cnt_in", [8, 1], F32, kind="Internal").ap()
            self.c_cnt_out = dt_("c_cnt_out", [64, 1], F32, kind="Internal").ap()
            self.c_cnt2_in = dt_("c_cnt2_in", [8, 1], F32, kind="Internal").ap()
            self.c_cnt2_out = dt_("c_cnt2_out", [64, 1], F32, kind="Internal").ap()
            self.lin_f = dt_("lin_f", [8192], F32, kind="Internal").ap()
            self.c_val_in = dt_("c_val_in", [8, T], I16, kind="Internal").ap()
            self.c_val_out = dt_("c_val_out", [64, T], I16, kind="Internal").ap()
            self.v2d = dt_("v2d", [8, T], I16, kind="Internal").ap()
            self.c_yb_in = dt_("c_yb_in", [CAP, D], BF16, kind="Internal").ap()
            self.c_yb_out = dt_("c_yb_out", [8 * CAP, D], BF16, kind="Internal").ap()
            self.vtm = dt_("vtm", [T, 128], I16, kind="Internal").ap()
            self.h_spill = dt_("h_spill", [32, 128, CAP], BF16, kind="Internal").ap()
        if self.upto >= 3:
            self.caw = dt_("caw", [D, 3 * D], F32, kind="ExternalInput").ap()
            self.cab = dt_("cab", [1, 3 * D], F32, kind="ExternalInput").ap()
            self.cpw = dt_("cpw", [D, D], F32, kind="ExternalInput").ap()
            self.cpb = dt_("cpb", [1, D], F32, kind="ExternalInput").ap()
        if self.upto >= 4:
            self.gwT = dt_("gwT", [D, 8], F32, kind="ExternalInput").ap()
            self.wgT = dt_("wgT", [8, D, 512], F32, kind="ExternalInput").ap()
            self.wuT = dt_("wuT", [8, D, 512], F32, kind="ExternalInput").ap()
            self.wdT = dt_("wdT", [8, 512, D], F32, kind="ExternalInput").ap()
            self.wsgT = dt_("wsgT", [D, D], F32, kind="ExternalInput").ap()
            self.wsuT = dt_("wsuT", [D, D], F32, kind="ExternalInput").ap()
            self.wsdT = dt_("wsdT", [D, D], F32, kind="ExternalInput").ap()

        sb = nc.alloc_sbuf_tensor
        self.xt = [sb(f"xt{i}", [128, D], F32) for i in range(2)]
        self.xnt_tile = [sb(f"xnt_t{i}", [128, D], BF16) for i in range(2)]
        self.sqtrash = sb("sqtrash", [128, D], BF16)
        self.small = sb("small", [128, 16], F32)
        self.av_t = [sb(f"av_t{i}", [128, T], BF16) for i in range(2)]
        self.zinv = [sb(f"zinv{i}", [128, 16], F32) for i in range(2)]
        self.nrm = [sb(f"nrm{i}", [128, D], F32) for i in range(2)]
        self.wbuf = [sb(f"wbuf{i}", [128, NCH, 256], BF16) for i in range(2)]
        self.wvbuf = sb("wvbuf", [128, NCH, 256], BF16)
        if self.upto >= 2:
            self.rw_sb = sb("rw_sb", [128, NCH, 8], BF16)
            self.logits = sb("logits", [128, NG, 8], F32)
            self.OHf = sb("OHf", [128, NG, 8], F32)
            self.OHb = sb("OHb", [128, NG, 8], BF16)
            self.base = sb("base", [128, NG, 8], F32)
            self.val_sb = sb("val_sb", [128, NG, 128], F32)
            self.rcol = {k: sb("rc_" + k, [128, NG], F32) for k in
                         ["i1", "i2", "w1", "w2", "k0", "k1", "gw0", "gw1",
                          "pos0", "pos1", "sl0", "sl1", "t0", "t1", "gid"]}
            self.sl0_i = sb("sl0_i", [128, NG], I32)
            self.sl1_i = sb("sl1_i", [128, NG], I32)
            self.r8 = {k: sb("r8_" + k, [128, 8], F32) for k in
                       ["oh1", "oh2", "xm", "tmp"]}
            self.iotaE = sb("iotaE", [128, 8], F32)
            self.UTc = sb("UTc", [128, 128], BF16)
            self.SLTc = sb("SLTc", [128, 128], F32)
            self.SLAc = sb("SLAc", [128, 8], F32)
            self.ones_c = sb("ones_c", [128, 1], BF16)
            self.ones_f = sb("ones_f", [1, 128], F32)
            self.grow = sb("grow", [1, D], F32)
            self.totals = sb("totals", [128, 1], F32)
            self.offs_c = sb("offs_c", [128, 1], F32)
            self.offrow = sb("offrow", [1, 128], F32)
            self.corow = sb("corow", [1, 8], F32)
            self.cnt_sb = sb("cnt_sb", [64, 1], F32)
            self.sel_sb = sb("sel_sb", [64, 8], F32)
            self.offs_b = sb("offs_b", [128, 128], F32)
            self.coff_b = sb("coff_b", [128, 8], F32)
            self.I128f = sb("I128f", [128, 128], F32)
            self.cand = sb("cand", [16, 1024], I16)
            self.sgout = sb("sgout", [128, CAP // 16], I16)
            self.nf_sb = sb("nf_sb", [1, 1], U32)
            self.hsb = [sb(f"hsb{i}", [128, 512], BF16) for i in range(2)]
            self.sgsb = [sb(f"sgsb{i}", [128, 512], BF16) for i in range(2)]
        AR = 131584
        self.arena = sb("arena", [128, AR], mybir.dt.uint8)
        a0 = self._addr(self.arena)
        at = nc.alloc_sbuf_tensor_at
        self.XNT = at("XNT", [128, NCH, T], BF16, offset=a0 + 0)
        self.PT = [at(f"PT{i}", [128, T], BF16, offset=a0 + i * 4096) for i in range(2)]
        self.QT = at("QT", [128, NCH, T], BF16, offset=a0 + 32768)
        self.KT = at("KT", [128, NCH, T], BF16, offset=a0 + 65536)
        self.VA = at("VA", [128, NG, NH * 65], BF16, offset=a0 + 98304)
        self.ot = [at(f"ot{i}", [128, 1152], BF16, offset=a0 + i * 2304)
                   for i in range(2)]
        if self.upto >= 2:
            self.vT_sb = at("vT_sb", [128, T], I16, offset=a0 + 8192)
            self.vg = at("vg", [8, T], I16, offset=a0 + 12288)
            self.cand_f = at("cand_f", [16, 1024], F32, offset=a0 + 16384)
            self.vf = at("vf", [128, NCH, 16], F32, offset=a0 + 20480)
            self.vrow_sb = at("vrow_sb", [8, 1], I32, offset=a0 + 21056)
            self.sgr = at("sgr", [16, 128], I16, offset=a0 + 21120)
            self.sgrf = at("sgrf", [16, 128], F32, offset=a0 + 21440)
            self.linp_f = at("linp_f", [128, 3, 16], F32, offset=a0 + 22016)
            self.linp_i = at("linp_i", [128, 3, 16], I32, offset=a0 + 22272)
            self.koff_f = at("koff_f", [8, 1], F32, offset=a0 + 22528)
            self.koff_i = at("koff_i", [8, 1], I32, offset=a0 + 22592)
            self.strm_f = at("strm_f", [16, 320], F32, offset=a0 + 22656)
            self.OHKb = at("OHKb", [128, NG, 8], BF16, offset=a0 + 24064)
            self.selr_sb = at("selr_sb", [64, 8], F32, offset=a0 + 24320)
            self.cnt2_sb = at("cnt2_sb", [64, 1], F32, offset=a0 + 26368)
            self.sel0f = at("sel0f", [128, NG, 8], F32, offset=a0 + 27392)
            self.sel1f = at("sel1f", [128, NG, 8], F32, offset=a0 + 27904)
            self.linb = at("linb", [128, 16], F32, offset=a0 + 28416)
        if self.upto >= 3:
            self.PL1 = [at(f"PL1_{i}", [128, 5, 512], BF16, offset=a0 + i * 5120)
                        for i in range(2)]
            self.VSA = at("VSA", [64, NH * 65], BF16, offset=a0 + 10240)
            self.bvb = sb("bvb", [128, D], BF16)
            self.cpbb = sb("cpbb", [128, D], BF16)
            self.bias_sb = sb("bias_sb", [128, 24], F32)
            self.cpw_sb = at("cpw_sb", [128, NCH, D], BF16, offset=a0 + 32768)
            self.DTRI = sb("DTRI", [128, 128], BF16)
            self.SMSK = sb("SMSK", [64, 4, 512], BF16)
            self.KSTR = sb("KSTR", [128, NCH, 64], BF16)
        if self.upto >= 4:
            self.wgu = [at(f"wgu{i}", [128, NCH, 1024], BF16,
                           offset=a0 + 32768 + i * 16384) for i in range(2)]
            self.acc = at("acc", [128, 8, D], F32, offset=a0 + 65536)
            self.wd = [at(f"wd{i}", [128, 4, D], BF16,
                          offset=a0 + 98304 + i * 8192) for i in range(2)]
            self.hT = [at(f"hT{i}", [128, 4, 128], BF16,
                          offset=a0 + 114688 + i * 1024) for i in range(2)]
            self.I128b = sb("I128b", [128, 128], BF16)
        if self.upto >= 2:
            self.xbT = at("xbT", [128, 10, NCH, 512], BF16, offset=a0 + 32768)
            self.w3h = at("w3h", [128, 32, 512], BF16, offset=a0 + 0)
            self.hct = [at(f"hct{i}", [128, 32, 512], BF16,
                           offset=a0 + 32768 + i * 32768) for i in range(2)]
            self.ybe = [at(f"ybe{i}", [128, D], BF16, offset=a0 + 98304 + i * 2048)
                        for i in range(2)]
            self.rows0 = [at(f"rows0_{i}", [128, D], BF16, offset=a0 + 102400 + i * 2048)
                          for i in range(2)]
            self.rows1 = [at(f"rows1_{i}", [128, D], BF16, offset=a0 + 106496 + i * 2048)
                          for i in range(2)]
            self.gb_b = at("gb_b", [128, D], F32, offset=a0 + 110592)
            self.bb_b = at("bb_b", [128, D], F32, offset=a0 + 114688)
        self.PS = nc.alloc_psum_tensor("PS", [128, 4096], F32)

        self.TE = Sem(nc, "TE")
        self.VE = Sem(nc, "VE")
        self.SE = Sem(nc, "SE")
        self.GE = Sem(nc, "GE")
        self.sLNLD = Stream(nc, "lnld", 2)
        self.sLNST = Stream(nc, "lnst", 2)
        self.sTR = Stream(nc, "tr", 1)
        self.sW = Stream(nc, "w", 2)
        self.sAV = Stream(nc, "avst", 2)
        self.sOT = Stream(nc, "otr", 2)
        self.sXL = Stream(nc, "xld", 2)
        self.sOS = Stream(nc, "ost", 2)
        self.sC = Stream(nc, "cst", 1)
        self.sG = Stream(nc, "grw", 1)
        self.sCT = Stream(nc, "cnt", 1)
        self.sSL = Stream(nc, "slt", 4)
        self.sGA = Stream(nc, "gth", 1)
        self.sYB = Stream(nc, "ybs", 4)
        self.sSL2 = Stream(nc, "sl2", 2)
        self.sIG = Stream(nc, "ig", 2)
        self.sSLOT = Stream(nc, "slot", 2)
        self.sRP = Stream(nc, "rp", 1)
        self.sKF = Stream(nc, "kf", 1)
        self.sWD = Stream(nc, "wdl", 2)
        self.block = None

    def _addr(self, th):
        mls = self.nc.lookup_mls(th)
        return mls.addrs[0] if mls.addrs else mls.memorylocations[0].addr

    def bank(self, b, w=1):
        return self.PS[:, b * 512:(b + w) * 512]

    def ph_ln(self, src_dram, dst_stg, src_toks=None, src_stream=None):
        blk = self.block
        sm = self.small
        xt, xnt = self.xt, self.xnt_tile
        ve0, se0 = self.VE.n, self.SE.n
        ld_tok = [None] * NG
        st_tok = self.sLNST.predict(list(range(NG)))

        @blk.sync
        def _(g):
            g.wait_ge(self.VE.h, ve0)
            g.wait_ge(self.SE.h, se0)
            for gi in range(NG):
                if gi >= 2:
                    g.wait_ge(self.SE.h, se0 + (gi - 2) * 2 + 1)
                    g.wait_ge(self.VE.h, ve0 + (gi - 2) * 5 + 5)
                if src_toks is not None:
                    src_stream.wait(g, src_toks[gi])
                ld_tok[gi] = self.sLNLD.issue(
                    g.dma_start(out=xt[gi % 2][:, :],
                                in_=src_dram[gi * 128:(gi + 1) * 128, :]), gi)
                if gi >= 1:
                    g.wait_ge(self.VE.h, ve0 + (gi - 1) * 5 + 5)
                    tk = self.sLNST.issue(
                        g.dma_start(out=dst_stg[(gi - 1) * 128:gi * 128, :],
                                    in_=xnt[(gi - 1) % 2][:, :]), gi - 1)
                    assert tk == st_tok[gi - 1]
            g.wait_ge(self.VE.h, ve0 + (NG - 1) * 5 + 5)
            tk = self.sLNST.issue(
                g.dma_start(out=dst_stg[(NG - 1) * 128:NG * 128, :],
                            in_=xnt[(NG - 1) % 2][:, :]), NG - 1)
            assert tk == st_tok[NG - 1]

        @blk.vector
        def _(v):
            for gi in range(NG):
                t = xt[gi % 2]
                self.sLNLD.wait(v, ld_tok[gi])
                if gi >= 1:
                    v.wait_ge(self.SE.h, se0 + (gi - 1) * 2 + 1)
                self.VE.inc(v.reduce_sum(sm[:, 0:1], t[:, :], axis=AX.X, negate=True))
                v.wait_ge(self.VE.h, self.VE.n)
                self.VE.inc(v.tensor_scalar(sm[:, 1:2], sm[:, 0:1], 1.0 / D, None, ALU.mult))
                v.wait_ge(self.SE.h, se0 + gi * 2 + 1)
                self.VE.inc(v.tensor_scalar(sm[:, 3:4], sm[:, 2:3], 1.0 / D, EPS, ALU.mult, ALU.add))
                v.wait_ge(self.SE.h, se0 + gi * 2 + 2)
                self.VE.inc(v.reciprocal(sm[:, 5:6], sm[:, 4:5]))
                v.wait_ge(self.VE.h, self.VE.n)
                if gi >= 2:
                    self.sLNST.wait(v, st_tok[gi - 2])
                self.VE.inc(v.tensor_scalar(xnt[gi % 2][:, :], t[:, :], sm[:, 1:2], sm[:, 5:6],
                                            ALU.add, ALU.mult))
        assert self.VE.n == ve0 + 5 * NG

        @blk.scalar
        def _(s):
            for gi in range(NG):
                s.wait_ge(self.VE.h, ve0 + gi * 5 + 2)
                self.SE.inc(s.activation(out=self.sqtrash[:, :], in_=xt[gi % 2][:, :],
                                         func=AF.Square, bias=sm[:, 1:2], scale=1.0,
                                         accum_out=sm[:, 2:3]))
                s.wait_ge(self.VE.h, ve0 + gi * 5 + 3)
                self.SE.inc(s.activation(out=sm[:, 4:5], in_=sm[:, 3:4], func=AF.Sqrt))
        assert self.SE.n == se0 + 2 * NG

        self.ln_store_tok = [st_tok[NG - 2], st_tok[NG - 1]]

    def ph_transpose_tm_to_fm(self, src_stg, dst, nch):
        blk = self.block
        holder = [None]

        @blk.sync
        def _(g):
            for t in self.ln_store_tok:
                self.sLNST.wait(g, t)
            for c in range(nch):
                holder[0] = self.sTR.issue(
                    g.dma_start_transpose(dst[:, c, :], src_stg[:, c * 128:(c + 1) * 128]), 0)
        self.xnt_tok = holder[0]

    def ph_qkv_l0(self):
        blk = self.block
        wq, wv = self.wbuf, self.wvbuf
        te0, ve0 = self.TE.n, self.VE.n
        n_qk_loads = 16
        w_tok = [None] * 20

        @blk.gpsimd
        def _(g):
            for i in range(n_qk_loads):
                proj, ec = divmod(i, NCH)
                if i >= 2:
                    g.wait_ge(self.TE.h, te0 + (i - 1) * 4)
                w_tok[i] = self.sW.issue(g.dma_start(
                    out=wq[i % 2][:, :, 0:128],
                    in_=self.wqkv[proj, :, ec * 128:(ec + 1) * 128]
                        .rearrange("(c p) e -> p c e", p=128)), i)
            for hf in range(4):
                g.wait_ge(self.TE.h, te0 + 64 + hf * 16 if hf else te0 + 60)
                w_tok[16 + hf] = self.sW.issue(g.dma_start(
                    out=wv[:, :, :],
                    in_=self.wqkv[2, :, hf * 256:(hf + 1) * 256]
                        .rearrange("(c p) e -> p c e", p=128)), hf)

        @blk.tensor
        def _(t):
            self.sTR.wait(t, self.xnt_tok)
            ng = 0
            for i in range(n_qk_loads):
                self.sW.wait(t, w_tok[i])
                for qc in range(4):
                    b = ng % 8
                    if ng >= 8:
                        t.wait_ge(self.VE.h, ve0 + 1 + (ng - 8) + 1)
                    for dc in range(NCH):
                        ins = t.matmul(self.bank(b), wq[i % 2][:, dc, 0:128],
                                       self.XNT[:, dc, qc * 512:(qc + 1) * 512],
                                       start=(dc == 0), stop=(dc == NCH - 1))
                    self.TE.inc(ins)
                    ng += 1
            for hf in range(4):
                self.sW.wait(t, w_tok[16 + hf])
                for tc in range(NG):
                    b = ng % 8
                    if ng >= 8:
                        t.wait_ge(self.VE.h, ve0 + 1 + (ng - 8) + 1)
                    for dc in range(NCH):
                        ins = t.matmul(self.PS[:, (b % 8) * 512:(b % 8) * 512 + 256],
                                       self.XNT[:, dc, tc * 128:(tc + 1) * 128],
                                       wv[:, dc, :],
                                       start=(dc == 0), stop=(dc == NCH - 1))
                    self.TE.inc(ins)
                    ng += 1
        assert self.TE.n == te0 + 128

        @blk.vector
        def _(v):
            self.VE.inc(v.memset(self.VA[:, :, 64::65], 1.0))
            ng = 0
            for i in range(n_qk_loads):
                proj, ec = divmod(i, NCH)
                dstT = self.QT if proj == 0 else self.KT
                for qc in range(4):
                    v.wait_ge(self.TE.h, te0 + ng + 1)
                    self.VE.inc(v.tensor_copy(dstT[:, ec, qc * 512:(qc + 1) * 512],
                                              self.bank(ng % 8)))
                    ng += 1
            for hf in range(4):
                for tc in range(NG):
                    v.wait_ge(self.TE.h, te0 + ng + 1)
                    dst = self.VA[:, tc, hf * 260:(hf + 1) * 260]
                    dst = dst.rearrange("p (h j) -> p h j", h=4)[:, :, 0:64]
                    src = self.PS[:, (ng % 8) * 512:(ng % 8) * 512 + 256] \
                        .rearrange("p (h j) -> p h j", h=4)
                    self.VE.inc(v.tensor_copy(dst, src))
                    ng += 1
        self.qkv_ev_last = self.VE.n

    def ph_heads_l0(self):
        blk = self.block
        te0, se0, ve0 = self.TE.n, self.SE.n, self.VE.n
        NU = NH * NG * 2
        s_done, av_done, e_done = {}, {}, {}
        te = te0
        for h in range(NH):
            for ul in range(32):
                u = h * 32 + ul
                te += 1
                s_done[u] = te
                if ul > 0:
                    te += 1
                    av_done[u - 1] = te
            te += 1
            av_done[h * 32 + 31] = te
        se = se0
        for u in range(NU):
            se += 1
            e_done[u] = se
        ev_done = {h: ve0 + 1 + h + 1 for h in range(NH)}
        flat = self.sAV.predict([p for h in range(NH) for p in (h, h)])
        av_tok = [[flat[2 * h], flat[2 * h + 1]] for h in range(NH)]
        zpad_tok = self.sTR.predict([0])[0]

        @blk.tensor
        def _(t):
            t.wait_ge(self.VE.h, self.qkv_ev_last)
            for h in range(NH):
                kA = (h % 2) * 64
                ch = h // 2
                for ul in range(32):
                    u = h * 32 + ul
                    kk, qh = divmod(ul, 2)
                    if u >= 2:
                        t.wait_ge(self.SE.h, e_done[u - 2])
                    pb = (u % 2) * 2
                    for sc in range(2):
                        q0 = qh * 1024 + sc * 512
                        ins = t.matmul(self.bank(pb + sc),
                                       self.KT[kA:kA + 64, ch, kk * 128:(kk + 1) * 128],
                                       self.QT[kA:kA + 64, ch, q0:q0 + 512],
                                       start=True, stop=True)
                    self.TE.inc(ins)
                    assert self.TE.n == s_done[u]
                    if ul > 0:
                        up = u - 1
                        kkp, qhp = divmod(ul - 1, 2)
                        t.wait_ge(self.SE.h, e_done[up])
                        if kkp == 0 and h > 0:
                            t.wait_ge(self.VE.h, ev_done[h - 1])
                        for sc in range(2):
                            qc = qhp * 2 + sc
                            ins = t.matmul(
                                self.PS[0:65, 2048 + qc * 512:2048 + (qc + 1) * 512],
                                self.VA[:, kkp, h * 65:(h + 1) * 65],
                                self.PT[(up // 2) % 2][:, qc * 512:(qc + 1) * 512],
                                start=(kkp == 0), stop=(kkp == NG - 1))
                        self.TE.inc(ins)
                        assert self.TE.n == av_done[up]
                up = h * 32 + 31
                t.wait_ge(self.SE.h, e_done[up])
                for sc in range(2):
                    qc = 2 + sc
                    ins = t.matmul(self.PS[0:65, 2048 + qc * 512:2048 + (qc + 1) * 512],
                                   self.VA[:, NG - 1, h * 65:(h + 1) * 65],
                                   self.PT[(up // 2) % 2][:, qc * 512:(qc + 1) * 512],
                                   start=False, stop=True)
                self.TE.inc(ins)
                assert self.TE.n == av_done[up]

        @blk.scalar
        def _(s):
            for u in range(NU):
                qh = u % 2
                s.wait_ge(self.TE.h, s_done[u])
                if u >= 4:
                    s.wait_ge(self.TE.h, av_done[u - 4])
                pb = (u % 2) * 2
                ins = s.activation(out=self.PT[(u // 2) % 2][:, qh * 1024:(qh + 1) * 1024],
                                   in_=self.bank(pb, 2), func=AF.Exp, scale=0.125)
                self.SE.inc(ins)
                assert self.SE.n == e_done[u]

        @blk.vector
        def _(v):
            self.VE.inc(v.memset(self.av_t[0][:, :], 0.0))
            for h in range(NH):
                v.wait_ge(self.TE.h, av_done[h * 32 + 31])
                if h == 0:
                    self.sTR.wait(v, zpad_tok)
                if h >= 2:
                    self.sAV.wait(v, av_tok[h - 2][1])
                ins = v.tensor_copy(self.av_t[h % 2][0:65, :], self.PS[0:65, 2048:4096])
                self.VE.inc(ins)
                assert self.VE.n == ev_done[h]

        @blk.sync
        def _(g):
            g.wait_ge(self.VE.h, ve0 + 1)
            tk = self.sTR.issue(
                g.dma_start(out=self.stg_fm[1040:1152, :], in_=self.av_t[0][0:112, :]), 0)
            assert tk == zpad_tok
            for h in range(NH):
                g.wait_ge(self.VE.h, ev_done[h])
                t0 = self.sAV.issue(
                    g.dma_start(out=self.stg_fm[h * 64:(h + 1) * 64, :],
                                in_=self.av_t[h % 2][0:64, :]), h)
                t1 = self.sAV.issue(
                    g.dma_start(out=self.stg_fm[1024 + h:1025 + h, :],
                                in_=self.av_t[h % 2][64:65, :]), h)
                assert [t0, t1] == av_tok[h]
        self.heads_fm_tok = [av_tok[NH - 2][1], av_tok[NH - 1][1]]

    def ph_finish_attn_l0(self):
        blk = self.block
        ve0 = self.VE.n
        tr_tok = [None] * NG
        x_tok = [None] * NG
        st_tok = self.sOS.predict(list(range(NG)))

        def emit_store(g, gi):
            g.wait_ge(self.VE.h, ve0 + gi * 3 + 3)
            dst = self.out_x if self.upto == 1 else self.x_dram
            tk = self.sOS.issue(
                g.dma_start(out=dst[gi * 128:(gi + 1) * 128, :],
                            in_=self.nrm[gi % 2][:, :]), gi)
            assert tk == st_tok[gi]

        @blk.sync
        def _(g):
            for t in self.heads_fm_tok:
                self.sAV.wait(g, t)
            for gi in range(NG):
                if gi >= 2:
                    g.wait_ge(self.VE.h, ve0 + (gi - 2) * 3 + 2)
                tr_tok[gi] = self.sOT.issue(
                    g.dma_start_transpose(self.ot[gi % 2][:, :],
                                          self.stg_fm[:, gi * 128:(gi + 1) * 128]), gi)
                if gi >= 1:
                    emit_store(g, gi - 1)
            emit_store(g, NG - 1)

        @blk.scalar
        def _(s):
            for gi in range(NG):
                if gi >= 2:
                    s.wait_ge(self.VE.h, ve0 + (gi - 2) * 3 + 3)
                x_tok[gi] = self.sXL.issue(
                    s.dma_start(out=self.xt[gi % 2][:, :],
                                in_=self.x_in[gi * 128:(gi + 1) * 128, :]), gi)

        @blk.vector
        def _(v):
            for gi in range(NG):
                o = self.ot[gi % 2]
                self.sOT.wait(v, tr_tok[gi])
                self.sXL.wait(v, x_tok[gi])
                if gi >= 2:
                    self.sOS.wait(v, st_tok[gi - 2])
                self.VE.inc(v.reciprocal(self.zinv[gi % 2][:, :], o[:, 1024:1040]))
                v.wait_ge(self.VE.h, self.VE.n)
                zb = self.zinv[gi % 2].ap().rearrange("p (h o) -> p h o", o=1) \
                    .broadcast_to([128, 16, 64])
                self.VE.inc(v.tensor_tensor(
                    self.nrm[gi % 2].ap().rearrange("p (h j) -> p h j", h=16),
                    o[:, 0:1024].rearrange("p (h j) -> p h j", h=16),
                    zb, ALU.mult))
                v.wait_ge(self.VE.h, self.VE.n)
                self.VE.inc(v.tensor_add(self.nrm[gi % 2][:, :], self.nrm[gi % 2][:, :],
                                         self.xt[gi % 2][:, :]))

        self.final_tok = [st_tok[NG - 2], st_tok[NG - 1]]

    def ph_consts_l0ffn(self, wqkv_like=True):
        nc, blk = self.nc, self.block
        iota_np = np.tile(np.arange(8, dtype=np.float32), (128, 1))
        ut_np = np.triu(np.ones((128, 128), dtype=np.float32)).astype(np.float32)
        gg, ee = np.divmod(np.arange(128), 8)
        slt_np = ((ee[:, None] == ee[None, :]) & (gg[:, None] < gg[None, :])).astype(np.float32)
        sla_np = (ee[:, None] == np.arange(8)[None, :]).astype(np.float32)
        i128_np = np.eye(128, dtype=np.float32)
        c_iota = nc.inline_tensor(iota_np, name="c_iota")
        c_ut = nc.inline_tensor(ut_np.astype(np.float32), name="c_ut")
        c_slt = nc.inline_tensor(slt_np, name="c_slt")
        c_sla = nc.inline_tensor(sla_np, name="c_sla")
        c_i128 = nc.inline_tensor(i128_np, name="c_i128")
        tok = {}

        @blk.gpsimd
        def _(g):
            for name, dst, srcap in [
                    ("iota", self.iotaE[:, :], c_iota.ap()),
                    ("ut", self.UTc[:, :], c_ut.ap()),
                    ("slt", self.SLTc[:, :], c_slt.ap()),
                    ("sla", self.SLAc[:, :], c_sla.ap()),
                    ("i128", self.I128f[:, :], c_i128.ap()),
                    ("rw", self.rw_sb[:, :, :], self.rwT.rearrange("(c p) e -> p c e", p=128)),
                    ("sel", self.sel_sb[:, :], self.sel_in),
                    ("gid", self.rcol["gid"][:, :], self.gidp1_in)]:
                tok[name] = self.sC.issue(g.dma_start(out=dst, in_=srcap), 0)
            tok["all"] = tok["gid"]

        @blk.vector
        def _(v):
            self.VE.inc(v.memset(self.ones_c[:, :], 1.0))
            self.VE.inc(v.memset(self.ones_f[:, :], 1.0))
            self.VE.inc(v.memset(self.val_sb[:, :, :], -1.0))
        self.c_tok = tok
        self.c_ve = self.VE.n

    def ph_ln2(self):
        blk = self.block
        sm = self.small
        xt, xnt = self.xt, self.xnt_tile
        te0 = self.TE.n
        gtok = {}

        @blk.sync
        def _(g):
            gtok['grow'] = self.sG.issue(g.dma_start(out=self.grow[:, :], in_=self.g_gamma), 0)

        @blk.tensor
        def _(t):
            self.sG.wait(t, gtok['grow'])
            t.wait_ge(self.VE.h, self.c_ve)
            for half in range(2):
                ins = t.matmul(self.bank(half), self.ones_f[:, :],
                               self.grow[:, half * 512:(half + 1) * 512],
                               start=True, stop=True)
            self.TE.inc(ins)

        @blk.vector
        def _(v):
            v.wait_ge(self.TE.h, te0 + 1)
            self.VE.inc(v.tensor_copy(self.gb_b[:, 0:512], self.bank(0)))
            self.VE.inc(v.tensor_copy(self.gb_b[:, 512:1024], self.bank(1)))
        gb_ve = self.VE.n

        @blk.sync
        def _(g):
            g.wait_ge(self.VE.h, gb_ve)
            gtok['brow'] = self.sG.issue(g.dma_start(out=self.grow[:, :], in_=self.g_beta), 0)

        @blk.tensor
        def _(t):
            self.sG.wait(t, gtok['brow'])
            for half in range(2):
                ins = t.matmul(self.bank(half), self.ones_f[:, :],
                               self.grow[:, half * 512:(half + 1) * 512],
                               start=True, stop=True)
            self.TE.inc(ins)

        @blk.vector
        def _(v):
            v.wait_ge(self.TE.h, te0 + 2)
            self.VE.inc(v.tensor_copy(self.bb_b[:, 0:512], self.bank(0)))
            self.VE.inc(v.tensor_copy(self.bb_b[:, 512:1024], self.bank(1)))

        ve0, se0 = self.VE.n, self.SE.n
        ld_tok = [None] * NG
        st_tok = self.sLNST.predict(list(range(NG)))
        VPT = 7

        @blk.sync
        def _(g):
            g.wait_ge(self.VE.h, ve0)
            g.wait_ge(self.SE.h, se0)
            for gi in range(NG):
                if gi >= 2:
                    g.wait_ge(self.SE.h, se0 + (gi - 2) * 2 + 1)
                    g.wait_ge(self.VE.h, ve0 + (gi - 2) * VPT + VPT)
                ld_tok[gi] = self.sLNLD.issue(
                    g.dma_start(out=xt[gi % 2][:, :],
                                in_=self.x_dram[gi * 128:(gi + 1) * 128, :]), gi)
                if gi >= 1:
                    g.wait_ge(self.VE.h, ve0 + (gi - 1) * VPT + VPT)
                    tk = self.sLNST.issue(
                        g.dma_start(out=self.c_pool_in[(gi - 1) * 128:gi * 128, :],
                                    in_=xnt[(gi - 1) % 2][:, :]), gi - 1)
                    assert tk == st_tok[gi - 1]
            g.wait_ge(self.VE.h, ve0 + (NG - 1) * VPT + VPT)
            tk = self.sLNST.issue(
                g.dma_start(out=self.c_pool_in[(NG - 1) * 128:NG * 128, :],
                            in_=xnt[(NG - 1) % 2][:, :]), NG - 1)
            assert tk == st_tok[NG - 1]

        @blk.vector
        def _(v):
            for tk in self.final_tok:
                self.sOS.wait(v, tk)
            for gi in range(NG):
                t = xt[gi % 2]
                self.sLNLD.wait(v, ld_tok[gi])
                if gi >= 1:
                    v.wait_ge(self.SE.h, se0 + (gi - 1) * 2 + 1)
                self.VE.inc(v.reduce_sum(sm[:, 0:1], t[:, :], axis=AX.X, negate=True))
                v.wait_ge(self.VE.h, self.VE.n)
                self.VE.inc(v.tensor_scalar(sm[:, 1:2], sm[:, 0:1], 1.0 / D, None, ALU.mult))
                v.wait_ge(self.SE.h, se0 + gi * 2 + 1)
                self.VE.inc(v.tensor_scalar(sm[:, 3:4], sm[:, 2:3], 1.0 / D, EPS, ALU.mult, ALU.add))
                v.wait_ge(self.SE.h, se0 + gi * 2 + 2)
                self.VE.inc(v.reciprocal(sm[:, 5:6], sm[:, 4:5]))
                v.wait_ge(self.VE.h, self.VE.n)
                self.VE.inc(v.tensor_scalar(self.nrm[gi % 2][:, :], t[:, :], sm[:, 1:2], sm[:, 5:6],
                                            ALU.add, ALU.mult))
                v.wait_ge(self.VE.h, self.VE.n)
                self.VE.inc(v.tensor_mul(self.nrm[gi % 2][:, :], self.nrm[gi % 2][:, :],
                                         self.gb_b[:, :]))
                v.wait_ge(self.VE.h, self.VE.n)
                if gi >= 2:
                    self.sLNST.wait(v, st_tok[gi - 2])
                self.VE.inc(v.tensor_add(xnt[gi % 2][:, :], self.nrm[gi % 2][:, :],
                                         self.bb_b[:, :]))
        assert self.VE.n == ve0 + VPT * NG

        @blk.scalar
        def _(s):
            for gi in range(NG):
                s.wait_ge(self.VE.h, ve0 + gi * VPT + 2)
                self.SE.inc(s.activation(out=self.sqtrash[:, :], in_=xt[gi % 2][:, :],
                                         func=AF.Square, bias=sm[:, 1:2], scale=1.0,
                                         accum_out=sm[:, 2:3]))
                s.wait_ge(self.VE.h, ve0 + gi * VPT + 3)
                self.SE.inc(s.activation(out=sm[:, 4:5], in_=sm[:, 3:4], func=AF.Sqrt))

        @blk.gpsimd
        def _(g):
            for tk in (st_tok[NG - 2], st_tok[NG - 1]):
                self.sLNST.wait(g, tk)
            self.GE.inc(g.collective_compute(
                "AllGather", ALU.bypass, replica_groups=[list(range(NE))],
                ins=[self.c_pool_in[:, :]], outs=[self.c_pool_out[:, :]]))
        self.pool_ag_ge = self.GE.n
        self.ln2_store_tok = [st_tok[NG - 2], st_tok[NG - 1]]

    def ph_route(self):
        blk = self.block
        rc, r8 = self.rcol, self.r8
        sm = self.small
        ge_base = self.GE.n
        ag0_ge = ge_base + 1
        ag0b_ge = ge_base + 2
        a2a_ge = ge_base + 3
        cnt_w_tok, cnt_r_tok, cnt2_w_tok, cnt2_r_tok, selr_tok = self.sCT.predict([0] * 5)

        htok = [None]

        @blk.sync
        def _(g):
            for t in self.ln2_store_tok:
                self.sLNST.wait(g, t)
            for c in range(NCH):
                htok[0] = self.sTR.issue(
                    g.dma_start_transpose(self.XNT[:, c, :],
                                          self.c_pool_in[:, c * 128:(c + 1) * 128]), 0)
        xnt_tok = htok[0]

        te0, ve0, se0 = self.TE.n, self.VE.n, self.SE.n

        @blk.tensor
        def _(t):
            self.sTR.wait(t, xnt_tok)
            self.sC.wait(t, self.c_tok['all'])
            for tc in range(NG):
                if tc >= 2:
                    t.wait_ge(self.VE.h, ve0 + tc - 1)
                for dc in range(NCH):
                    ins = t.matmul(self.PS[:, tc % 2 * 512: tc % 2 * 512 + 8],
                                   self.XNT[:, dc, tc * 128:(tc + 1) * 128],
                                   self.rw_sb[:, dc, :],
                                   start=(dc == 0), stop=(dc == NCH - 1))
                self.TE.inc(ins)

        @blk.vector
        def _(v):
            for tc in range(NG):
                v.wait_ge(self.TE.h, te0 + tc + 1)
                self.VE.inc(v.tensor_scalar(self.logits[:, tc, :],
                                            self.PS[:, tc % 2 * 512: tc % 2 * 512 + 8],
                                            10000.0, -10000.0, ALU.min, ALU.max))

        ve1 = self.VE.n
        VPT = 17
        se1 = self.SE.n

        @blk.vector
        def _(v):
            for tc in range(NG):
                lg = self.logits[:, tc, :]
                def C(ins):
                    self.VE.inc(ins)
                    v.wait_ge(self.VE.h, self.VE.n)
                C(v.reduce_max(sm[:, 8:9], lg, axis=AX.X))
                C(v.tensor_scalar(r8["oh1"][:, :], lg, sm[:, 8:9], None, ALU.is_ge))
                C(v.scalar_tensor_tensor(r8["xm"][:, :], r8["oh1"][:, :], -2.0e30, lg,
                                         ALU.mult, ALU.add))
                C(v.reduce_max(sm[:, 9:10], r8["xm"][:, :], axis=AX.X))
                C(v.tensor_scalar(r8["oh2"][:, :], r8["xm"][:, :], sm[:, 9:10], None, ALU.is_ge))
                C(v.tensor_mul(r8["tmp"][:, :], r8["oh1"][:, :], self.iotaE[:, :]))
                C(v.reduce_sum(rc["i1"][:, tc:tc + 1], r8["tmp"][:, :], axis=AX.X))
                C(v.tensor_mul(r8["tmp"][:, :], r8["oh2"][:, :], self.iotaE[:, :]))
                C(v.reduce_sum(rc["i2"][:, tc:tc + 1], r8["tmp"][:, :], axis=AX.X))
                C(v.tensor_sub(sm[:, 10:11], sm[:, 9:10], sm[:, 8:9]))
                v.wait_ge(self.SE.h, se1 + tc + 1)
                C(v.tensor_scalar(sm[:, 12:13], sm[:, 11:12], 1.0 + 1e-12, None, ALU.add))
                C(v.reciprocal(rc["w1"][:, tc:tc + 1], sm[:, 12:13]))
                C(v.tensor_mul(rc["w2"][:, tc:tc + 1], sm[:, 11:12], rc["w1"][:, tc:tc + 1]))
                C(v.tensor_add(self.OHf[:, tc, :], r8["oh1"][:, :], r8["oh2"][:, :]))
                C(v.tensor_copy(self.OHb[:, tc, :], self.OHf[:, tc, :]))
                C(v.tensor_copy(self.val_sb[:, tc, 0:8], r8["oh1"][:, :]))
                C(v.tensor_copy(self.val_sb[:, tc, 8:16], r8["oh2"][:, :]))
                assert self.VE.n == ve1 + (tc + 1) * VPT

        @blk.scalar
        def _(s):
            for tc in range(NG):
                s.wait_ge(self.VE.h, ve1 + tc * VPT + 10)
                self.SE.inc(s.activation(out=sm[:, 11:12], in_=sm[:, 10:11], func=AF.Exp))

        te1, ve2 = self.TE.n, self.VE.n

        @blk.tensor
        def _(t):
            t.wait_ge(self.VE.h, ve1 + NG * VPT)
            for tc in range(NG):
                self.TE.inc(t.matmul(self.PS[:, 1024 + tc * 8: 1024 + (tc + 1) * 8],
                                     self.UTc[:, :],
                                     self.OHb[:, tc, :], start=True, stop=True))
            self.TE.inc(t.matmul(self.PS[:, 1536:1537],
                                 self.OHb.ap().rearrange("p g e -> p (g e)"),
                                 self.ones_c[:, :], start=True, stop=True))
            t.wait_ge(self.VE.h, ve2 + 2)
            self.TE.inc(t.matmul(self.PS[:, 1540:1541], self.SLTc[:, :],
                                 self.totals[:, :], start=True, stop=True))
            self.TE.inc(t.matmul(self.PS[0:8, 1544:1545], self.SLAc[:, :],
                                 self.totals[:, :], start=True, stop=True))
            t.wait_ge(self.VE.h, ve2 + 3)
            self.TE.inc(t.transpose(self.PS[0:1, 1664:1792], self.offs_c[:, :],
                                    self.I128f[:, :]))
            self.sCT.wait(t, cnt_r_tok)
            self.TE.inc(t.matmul(self.PS[0:1, 1800:1808], self.cnt_sb[:, :],
                                 self.sel_sb[:, :], start=True, stop=True))
            t.wait_ge(self.VE.h, ve2 + 6)
            self.TE.inc(t.matmul(self.PS[:, 2048:2176], self.ones_f[:, :],
                                 self.offrow[:, :], start=True, stop=True))
            self.TE.inc(t.matmul(self.PS[:, 2560:2568], self.ones_f[:, :],
                                 self.corow[:, :], start=True, stop=True))

        @blk.vector
        def _(v):
            for tc in range(NG):
                v.wait_ge(self.TE.h, te1 + NG)
            self.VE.inc(v.tensor_copy(
                self.base.ap().rearrange("p g e -> p (g e)"),
                self.PS[:, 1024:1152]))
            v.wait_ge(self.TE.h, te1 + NG + 1)
            self.VE.inc(v.tensor_copy(self.totals[:, :], self.PS[:, 1536:1537]))
            v.wait_ge(self.VE.h, self.VE.n)
            v.wait_ge(self.TE.h, te1 + NG + 2)
            self.VE.inc(v.tensor_copy(self.offs_c[:, :], self.PS[:, 1540:1541]))
            v.wait_ge(self.TE.h, te1 + NG + 3)
            self.VE.inc(v.tensor_copy(sm[:, 13:14], self.PS[:, 1544:1545]))
            v.wait_ge(self.TE.h, te1 + NG + 4)
            self.VE.inc(v.tensor_copy(self.offrow[:, :], self.PS[0:1, 1664:1792]))
            v.wait_ge(self.TE.h, te1 + NG + 5)
            self.VE.inc(v.tensor_copy(self.corow[:, :], self.PS[0:1, 1800:1808]))
            v.wait_ge(self.TE.h, te1 + NG + 7)
            self.VE.inc(v.tensor_copy(self.offs_b[:, :], self.PS[:, 2048:2176]))
            v.wait_ge(self.VE.h, self.VE.n)
            self.VE.inc(v.tensor_copy(self.coff_b[:, :], self.PS[:, 2560:2568]))
        cnts_ve = self.VE.n

        @blk.sync
        def _(g):
            g.wait_ge(self.VE.h, ve2 + 4)
            tk = self.sCT.issue(g.dma_start(out=self.c_cnt_in[:, :], in_=sm[0:8, 13:14]), 0)
            assert tk == cnt_w_tok
            g.wait_ge(self.GE.h, ag0_ge)
            tk = self.sCT.issue(g.dma_start(out=self.cnt_sb[:, :], in_=self.c_cnt_out), 0)
            assert tk == cnt_r_tok

        @blk.gpsimd
        def _(g):
            self.sCT.wait(g, cnt_w_tok)
            self.GE.inc(g.collective_compute(
                "AllGather", ALU.bypass, replica_groups=[list(range(NE))],
                ins=[self.c_cnt_in[:, :]], outs=[self.c_cnt_out[:, :]]))
            assert self.GE.n == ag0_ge

        ve3 = self.VE.n
        sqv = self.nrm[0].ap().rearrange("p (g e) -> p g e", e=8)[:, 0:NG, :]

        @blk.vector
        def _(v):
            def C(ins):
                self.VE.inc(ins)
                v.wait_ge(self.VE.h, self.VE.n)
            C(v.tensor_sub(self.base.ap().rearrange("p g e -> p (g e)"),
                           self.base.ap().rearrange("p g e -> p (g e)"),
                           self.OHf.ap().rearrange("p g e -> p (g e)")))
            C(v.tensor_add(self.base.ap().rearrange("p g e -> p (g e)"),
                           self.base.ap().rearrange("p g e -> p (g e)"),
                           self.offs_b[:, :]))
            C(v.tensor_tensor(self.base[:, :, :], self.base[:, :, :],
                              self.coff_b.ap().rearrange("p (o e) -> p o e", o=1)
                              .broadcast_to([128, NG, 8]), ALU.add))
            for ohslice, poskey in ((slice(0, 8), "pos0"), (slice(8, 16), "pos1")):
                C(v.tensor_mul(sqv, self.val_sb[:, :, ohslice], self.base[:, :, :]))
                C(v.reduce_sum(rc[poskey].ap().rearrange("p (g o) -> p g o", o=1),
                               sqv, axis=AX.X))
            C(v.tensor_scalar(rc["k0"][:, :], rc["pos0"][:, :], float(CAP), None, ALU.is_lt))
            C(v.tensor_scalar(rc["k1"][:, :], rc["pos1"][:, :], float(CAP), None, ALU.is_lt))
            C(v.tensor_mul(rc["gw0"][:, :], rc["w1"][:, :], rc["k0"][:, :]))
            C(v.tensor_mul(rc["gw1"][:, :], rc["w2"][:, :], rc["k1"][:, :]))
            C(v.tensor_tensor(self.val_sb[:, :, 0:8], self.val_sb[:, :, 0:8],
                              rc["k0"].ap().rearrange("p (g o) -> p g o", o=1)
                              .broadcast_to([128, NG, 8]), ALU.mult))
            C(v.tensor_tensor(self.val_sb[:, :, 8:16], self.val_sb[:, :, 8:16],
                              rc["k1"].ap().rearrange("p (g o) -> p g o", o=1)
                              .broadcast_to([128, NG, 8]), ALU.mult))
            C(v.tensor_copy(self.sel0f[:, :, :], self.val_sb[:, :, 0:8]))
            C(v.tensor_copy(self.sel1f[:, :, :], self.val_sb[:, :, 8:16]))
            C(v.tensor_add(self.val_sb[:, :, 0:8], self.val_sb[:, :, 0:8],
                           self.val_sb[:, :, 8:16]))
            C(v.tensor_copy(self.OHKb[:, :, :], self.val_sb[:, :, 0:8]))
            for tc in range(NG):
                C(v.tensor_scalar(self.val_sb[:, tc, 0:8], self.val_sb[:, tc, 0:8],
                                  rc["gid"][:, tc:tc + 1], -1.0, ALU.mult, ALU.add))
            C(v.memset(self.val_sb[:, :, 8:16], -1.0))
        self.route_ve = self.VE.n

        te2, ve4 = self.TE.n, self.VE.n
        @blk.tensor
        def _(t):
            t.wait_ge(self.VE.h, self.route_ve)
            for tc in range(NG):
                self.TE.inc(t.matmul(self.PS[:, 1024 + tc * 8: 1024 + (tc + 1) * 8],
                                     self.UTc[:, :],
                                     self.OHKb[:, tc, :], start=True, stop=True))
            self.TE.inc(t.matmul(self.PS[:, 1536:1537],
                                 self.OHKb.ap().rearrange("p g e -> p (g e)"),
                                 self.ones_c[:, :], start=True, stop=True))
            t.wait_ge(self.VE.h, ve4 + 2)
            self.TE.inc(t.matmul(self.PS[:, 1540:1541], self.SLTc[:, :],
                                 self.totals[:, :], start=True, stop=True))
            self.TE.inc(t.matmul(self.PS[0:8, 1544:1545], self.SLAc[:, :],
                                 self.totals[:, :], start=True, stop=True))
            t.wait_ge(self.VE.h, ve4 + 3)
            self.TE.inc(t.transpose(self.PS[0:1, 1664:1792], self.offs_c[:, :],
                                    self.I128f[:, :]))
            self.sCT.wait(t, selr_tok)
            self.TE.inc(t.matmul(self.PS[0:1, 1800:1808], self.cnt2_sb[:, :],
                                 self.sel_sb[:, :], start=True, stop=True))
            self.TE.inc(t.matmul(self.PS[0:1, 1816:1824], self.cnt2_sb[:, :],
                                 self.selr_sb[:, :], start=True, stop=True))
            t.wait_ge(self.VE.h, ve4 + 7)
            self.TE.inc(t.matmul(self.PS[:, 2048:2176], self.ones_f[:, :],
                                 self.offrow[:, :], start=True, stop=True))
            self.TE.inc(t.matmul(self.PS[:, 2560:2568], self.ones_f[:, :],
                                 self.corow[:, :], start=True, stop=True))
            t.wait_ge(self.VE.h, ve4 + 8)
            self.TE.inc(t.transpose(self.PS[0:8, 1832:1833],
                                    self.strm_f[0:1, 0:8], self.I128f[0:1, 0:1]))

        @blk.vector
        def _(v):
            v.wait_ge(self.TE.h, te2 + NG)
            self.VE.inc(v.tensor_copy(
                self.base.ap().rearrange("p g e -> p (g e)"),
                self.PS[:, 1024:1152]))
            v.wait_ge(self.TE.h, te2 + NG + 1)
            self.VE.inc(v.tensor_copy(self.totals[:, :], self.PS[:, 1536:1537]))
            v.wait_ge(self.VE.h, self.VE.n)
            v.wait_ge(self.TE.h, te2 + NG + 2)
            self.VE.inc(v.tensor_copy(self.offs_c[:, :], self.PS[:, 1540:1541]))
            v.wait_ge(self.TE.h, te2 + NG + 3)
            self.VE.inc(v.tensor_copy(sm[:, 13:14], self.PS[:, 1544:1545]))
            v.wait_ge(self.TE.h, te2 + NG + 4)
            self.VE.inc(v.tensor_copy(self.offrow[:, :], self.PS[0:1, 1664:1792]))
            v.wait_ge(self.TE.h, te2 + NG + 5)
            self.VE.inc(v.tensor_copy(self.corow[:, :], self.PS[0:1, 1800:1808]))
            v.wait_ge(self.TE.h, te2 + NG + 6)
            self.VE.inc(v.tensor_copy(self.strm_f[0:1, 0:8], self.PS[0:1, 1816:1824]))
            v.wait_ge(self.TE.h, te2 + NG + 8)
            self.VE.inc(v.tensor_copy(self.offs_b[:, :], self.PS[:, 2048:2176]))
            v.wait_ge(self.VE.h, self.VE.n)
            self.VE.inc(v.tensor_copy(self.coff_b[:, :], self.PS[:, 2560:2568]))
            v.wait_ge(self.TE.h, te2 + NG + 9)
            self.VE.inc(v.tensor_copy(self.koff_f[:, :], self.PS[0:8, 1832:1833]))

        @blk.sync
        def _(g):
            g.wait_ge(self.VE.h, ve4 + 4)
            tk = self.sCT.issue(g.dma_start(out=self.c_cnt2_in[:, :], in_=sm[0:8, 13:14]), 0)
            assert tk == cnt2_w_tok
            g.wait_ge(self.GE.h, ag0b_ge)
            tk = self.sCT.issue(g.dma_start(out=self.cnt2_sb[:, :], in_=self.c_cnt2_out), 0)
            assert tk == cnt2_r_tok
            tk = self.sCT.issue(g.dma_start(out=self.selr_sb[:, :], in_=self.selr_in), 0)
            assert tk == selr_tok

        @blk.gpsimd
        def _(g):
            self.sCT.wait(g, cnt2_w_tok)
            self.GE.inc(g.collective_compute(
                "AllGather", ALU.bypass, replica_groups=[list(range(NE))],
                ins=[self.c_cnt2_in[:, :]], outs=[self.c_cnt2_out[:, :]]))
            assert self.GE.n == ag0b_ge

        ve5 = self.VE.n

        @blk.vector
        def _(v):
            def C(ins):
                self.VE.inc(ins)
                v.wait_ge(self.VE.h, self.VE.n)
            v.wait_ge(self.VE.h, ve5)
            C(v.tensor_sub(self.base.ap().rearrange("p g e -> p (g e)"),
                           self.base.ap().rearrange("p g e -> p (g e)"),
                           self.OHKb.ap().rearrange("p g e -> p (g e)")))
            C(v.tensor_add(self.base.ap().rearrange("p g e -> p (g e)"),
                           self.base.ap().rearrange("p g e -> p (g e)"),
                           self.offs_b[:, :]))
            C(v.tensor_tensor(self.base[:, :, :], self.base[:, :, :],
                              self.coff_b.ap().rearrange("p (o e) -> p o e", o=1)
                              .broadcast_to([128, NG, 8]), ALU.add))
            for selb, poskey in ((self.sel0f, "pos0"), (self.sel1f, "pos1")):
                C(v.tensor_mul(sqv, selb[:, :, :], self.base[:, :, :]))
                C(v.reduce_sum(rc[poskey].ap().rearrange("p (g o) -> p g o", o=1),
                               sqv, axis=AX.X))
            C(v.tensor_scalar(rc["sl0"][:, :], rc["i1"][:, :], float(CAP), None, ALU.mult))
            C(v.tensor_add(rc["sl0"][:, :], rc["sl0"][:, :], rc["pos0"][:, :]))
            C(v.tensor_scalar(rc["sl1"][:, :], rc["i2"][:, :], float(CAP), None, ALU.mult))
            C(v.tensor_add(rc["sl1"][:, :], rc["sl1"][:, :], rc["pos1"][:, :]))
            for slk, kk_ in (("sl0", "k0"), ("sl1", "k1")):
                C(v.tensor_mul(rc[slk][:, :], rc[slk][:, :], rc[kk_][:, :]))
                C(v.tensor_scalar(rc["t0"][:, :], rc[kk_][:, :], -float(NE * CAP),
                                  float(NE * CAP), ALU.mult, ALU.add))
                C(v.tensor_add(rc[slk][:, :], rc[slk][:, :], rc["t0"][:, :]))
                C(v.tensor_scalar(rc[slk][:, :], rc[slk][:, :], float(NE * CAP - 1), None,
                                  ALU.min))
        self.route_ve2 = self.VE.n

        vtm_tok = self.sSL.predict([0])[0]
        sl0_tok, sl1_tok = self.sSLOT.predict([0, 1])
        cval_tok = self.sSL2.predict([0])[0]

        @blk.gpsimd
        def _(g):
            g.wait_ge(self.VE.h, self.route_ve)
            tk = self.sSL.issue(g.dma_start(
                out=self.vtm.rearrange("(g p) e -> p g e", p=128),
                in_=self.val_sb[:, :, :]), 0)
            assert tk == vtm_tok
            g.wait_ge(self.VE.h, self.route_ve2)
            tk = self.sSLOT.issue(g.dma_start(out=self.sl0_i[:, :], in_=rc["sl0"][:, :]), 0)
            assert tk == sl0_tok
            tk = self.sSLOT.issue(g.dma_start(out=self.sl1_i[:, :], in_=rc["sl1"][:, :]), 1)
            assert tk == sl1_tok
            tk = self.sKF.issue(g.dma_start(out=self.koff_i[:, :], in_=self.koff_f[:, :]), 0)
            self.koff_tok = tk

        vtt_holder = [None]

        @blk.sync
        def _(g):
            self.sSL.wait(g, vtm_tok)
            vtt_holder[0] = self.sTR.issue(
                g.dma_start_transpose(self.vT_sb[:, :], self.vtm[:, :]), 0)
            self.sTR.wait(g, vtt_holder[0])
            tk = self.sSL2.issue(g.dma_start(
                out=self.c_val_in, in_=self.vT_sb[0:8, :]), 0)
            assert tk == cval_tok

        @blk.gpsimd
        def _(g):
            self.sSL2.wait(g, cval_tok)
            self.GE.inc(g.collective_compute(
                "AllGather", ALU.bypass, replica_groups=[list(range(NE))],
                ins=[self.c_val_in[:, :]], outs=[self.c_val_out[:, :]]))
            assert self.GE.n == a2a_ge
        self.a2a_ge = a2a_ge
        self.slots_tok = [sl0_tok, sl1_tok]

    def ph_expert_in(self):
        blk = self.block
        from concourse.bass_core_rust import IndirectOffsetOnAxis
        te0, ve0, ge0 = self.TE.n, self.VE.n, self.GE.n
        vr_tok, vg_tok, vf_tok, candi_tok = self.sSL.predict([0, 1, 2, 3])
        v2d_tok = self.sSL2.predict([1])[0]

        @blk.gpsimd
        def _(g):
            tk = self.sSL.issue(g.dma_start(out=self.vrow_sb[:, :], in_=self.vrows_in), 0)
            assert tk == vr_tok
            self.sSL.wait(g, vr_tok)
            g.wait_ge(self.GE.h, self.a2a_ge)
            tk = self.sSL.issue(g.indirect_dma_start(
                out=self.vg[:, :], out_offset=None,
                in_=self.c_val_out[:, :],
                in_offset=IndirectOffsetOnAxis(self.vrow_sb[:, :], 0)), 1)
            assert tk == vg_tok

        @blk.sync
        def _(g):
            self.sSL.wait(g, vg_tok)
            tk = self.sSL2.issue(g.dma_start(out=self.v2d, in_=self.vg[:, :]), 1)
            assert tk == v2d_tok

        @blk.gpsimd
        def _(g):
            self.sSL2.wait(g, v2d_tok)
            tk = self.sSL.issue(g.dma_start(
                out=self.vf[:, :, :],
                in_=self.v2d.rearrange("r t -> (r t)")
                    .rearrange("(cb p a) -> p cb a", p=128, a=16)), 2)
            assert tk == vf_tok

        @blk.tensor
        def _(t):
            self.sSL.wait(t, vf_tok)
            for cb in range(8):
                self.TE.inc(t.transpose(self.PS[0:16, 3072 + cb * 128:3072 + (cb + 1) * 128],
                                        self.vf[:, cb, :], self.I128f[:, :]))

        @blk.vector
        def _(v):
            v.wait_ge(self.TE.h, te0 + 8)
            self.VE.inc(v.tensor_copy(self.cand_f[:, :], self.PS[0:16, 3072:4096]))

        ge1 = self.GE.n
        ge_r = {rr: ge1 + rr + 1 for rr in range(8)}
        te1 = self.TE.n
        te_r = {rr: te1 + rr + 1 for rr in range(8)}
        ve1 = self.VE.n + 1
        ve_r = {rr: ve1 + rr + 1 for rr in range(8)}
        sgrf_toks = {}
        NGC = 10
        gchunk_toks = self.sGA.predict(list(range(NGC)))

        zero_toks = self.sKF.predict([0, 0, 0, 0])

        @blk.vector
        def _(v):
            self.VE.inc(v.memset(self.linb[:, :], 0.0))
        zlin_ve = self.VE.n

        @blk.gpsimd
        def _(g):
            g.wait_ge(self.VE.h, ve0 + 1)
            tk = self.sSL.issue(g.dma_start(out=self.cand[:, :], in_=self.cand_f[:, :]), 3)
            assert tk == candi_tok
            self.sSL.wait(g, candi_tok)
            self.sKF.wait(g, self.koff_tok)
            g.wait_ge(self.VE.h, zlin_ve)
            for z in range(4):
                tk = self.sKF.issue(g.dma_start(out=self.lin_f[z * 2048:(z + 1) * 2048],
                                                in_=self.linb[:, :]), 0)
                assert tk == zero_toks[z]
            self.sKF.wait(g, tk)
            for rr in range(8):
                self.GE.inc(g.sparse_gather(out=self.sgr[:, :],
                                            in_=self.cand[:, 128 * rr:128 * (rr + 1)],
                                            num_found=self.nf_sb[:, :]))
                assert self.GE.n == ge_r[rr]
                g.wait_ge(self.GE.h, ge_r[rr])
                tk = self.sSL.issue(g.dma_start(out=self.sgrf[:, :], in_=self.sgr[:, :]), 0)
                sgrf_toks[rr] = tk
                g.wait_ge(self.VE.h, ve_r[rr])
                kreg = g.value_load(self.koff_i[rr:rr + 1, :], min_val=0, max_val=5120)
                from concourse.ap import AP as _AP
                dst = _AP(self.lin_f.tensor, kreg, [[1, 2048]])
                tk = self.sKF.issue(g.dma_start(out=dst, in_=self.linb[:, :]), 0)
                self.sKF.wait(g, tk)
            tk = self.sSL.issue(g.dma_start(
                out=self.linp_f[:, :, :],
                in_=self.lin_f[0:6144].rearrange("(m p a) -> p m a", m=3, a=16)), 2)
            self.lin_tok = tk

        @blk.tensor
        def _(t):
            for rr in range(8):
                t.wait_ge(self.GE.h, ge_r[rr])
                self.sSL.wait(t, sgrf_toks[rr])
                self.TE.inc(t.transpose(self.PS[0:128, 3072:3088],
                                        self.sgrf[:, :], self.I128f[0:16, 0:16]))
                assert self.TE.n == te_r[rr]

        @blk.vector
        def _(v):
            self.sKF.wait(v, zero_toks[-1])
            for rr in range(8):
                v.wait_ge(self.TE.h, te_r[rr])
                self.VE.inc(v.tensor_copy(self.linb[:, :], self.PS[:, 3072:3088]))
                assert self.VE.n == ve_r[rr]

        ve2 = self.VE.n
        te2 = self.TE.n
        strm_tok = self.sSL.predict([3])[0]
        rp_tok = self.sRP.predict([0] * 7)[-1]

        @blk.vector
        def _(v):
            self.sSL.wait(v, self.lin_tok)
            self.VE.inc(v.tensor_scalar(
                self.linp_f.ap().rearrange("p m a -> p (m a)"),
                self.linp_f.ap().rearrange("p m a -> p (m a)"),
                0.0, 16383.0, ALU.max, ALU.min))

        @blk.tensor
        def _(t):
            t.wait_ge(self.VE.h, ve2 + 1)
            for m in range(3):
                self.TE.inc(t.transpose(self.PS[0:16, 3200 + m * 128:3200 + (m + 1) * 128],
                                        self.linp_f[:, m, :], self.I128f[:, :]))

        @blk.vector
        def _(v):
            v.wait_ge(self.TE.h, te2 + 3)
            self.VE.inc(v.tensor_copy(self.strm_f[:, 0:256], self.PS[0:16, 3200:3456]))
            v.wait_ge(self.VE.h, self.VE.n)
            self.VE.inc(v.tensor_copy(self.strm_f[:, 256:320], self.PS[0:16, 3456:3520]))

        @blk.gpsimd
        def _(g):
            g.wait_ge(self.VE.h, ve2 + 3)
            tk = self.sSL.issue(g.dma_start(out=self.sgout[0:16, :], in_=self.strm_f[:, :]), 3)
            assert tk == strm_tok

        @blk.sync
        def _(g):
            self.sSL.wait(g, strm_tok)
            for grp in range(1, 8):
                tk = self.sRP.issue(g.dma_start(
                    out=self.sgout[16 * grp:16 * (grp + 1), :],
                    in_=self.sgout[0:16, :]), 0)
            assert tk == rp_tok

        @blk.gpsimd
        def _(g):
            self.sRP.wait(g, rp_tok)
            g.wait_ge(self.GE.h, self.pool_ag_ge)
            for k in range(NGC):
                ins = g.dma_gather(
                    out_ap=self.xbT[:, k, :, :],
                    in_ap=self.c_pool_out[:, :],
                    idxs_ap=self.sgout[:, 32 * k:32 * (k + 1)],
                    num_idxs=512, num_idxs_reg=512,
                    elem_size=D, transpose=True)
                tk = self.sGA.issue(ins, k)
                assert tk == gchunk_toks[k]
        self.xbT_tok = gchunk_toks[-1]

    def ph_ffn1(self):
        blk = self.block
        te0, se0, ve0 = self.TE.n, self.SE.n, self.VE.n
        NJ, NCT = 32, 10
        wj_tok = [None] * NJ
        h_tok = {}
        h_pred = self.sAV.predict([j * NCT + ct for j in range(NJ) for ct in range(NCT)])

        @blk.gpsimd
        def _(g):
            for j in range(NJ):
                if j >= 2:
                    g.wait_ge(self.TE.h, te0 + (j - 1) * NCT)
                self.sW.issue(g.dma_start(
                    out=self.wbuf[j % 2][:, :, 0:128],
                    in_=self.w12T[:, j * 128:(j + 1) * 128].rearrange("(c p) f -> p c f", p=128)), j)
                wj_tok[j] = self.sW.issue(g.dma_start(
                    out=self.wbuf[j % 2][:, :, 128:256],
                    in_=self.w12T[:, 4096 + j * 128: 4096 + (j + 1) * 128]
                        .rearrange("(c p) f -> p c f", p=128)), j)

        @blk.tensor
        def _(t):
            self.sGA.wait(t, self.xbT_tok)
            ng = 0
            for j in range(NJ):
                self.sW.wait(t, wj_tok[j])
                for ct in range(NCT):
                    pb = (ng % 2) * 2
                    if ng >= 2:
                        t.wait_ge(self.VE.h, ve0 + (ng - 2) + 1)
                        t.wait_ge(self.SE.h, se0 + (ng - 2) + 1)
                    for dc in range(NCH):
                        t.matmul(self.bank(pb), self.wbuf[j % 2][:, dc, 0:128],
                                 self.xbT[:, ct, dc, :],
                                 start=(dc == 0), stop=(dc == NCH - 1))
                    for dc in range(NCH):
                        ins = t.matmul(self.bank(pb + 1), self.wbuf[j % 2][:, dc, 128:256],
                                       self.xbT[:, ct, dc, :],
                                       start=(dc == 0), stop=(dc == NCH - 1))
                    self.TE.inc(ins)
                    ng += 1

        @blk.scalar
        def _(s):
            ng = 0
            for j in range(NJ):
                for ct in range(NCT):
                    pb = (ng % 2) * 2
                    s.wait_ge(self.TE.h, te0 + ng + 1)
                    if ng >= 2:
                        s.wait_ge(self.VE.h, ve0 + (ng - 2) + 1)
                    self.SE.inc(s.activation(out=self.sgsb[ng % 2][:, :],
                                             in_=self.bank(pb), func=AF.Silu))
                    ng += 1

        @blk.vector
        def _(v):
            ng = 0
            for j in range(NJ):
                for ct in range(NCT):
                    pb = (ng % 2) * 2
                    v.wait_ge(self.SE.h, se0 + ng + 1)
                    if ng >= 2:
                        self.sAV.wait(v, h_pred[ng - 2])
                    self.VE.inc(v.tensor_tensor(self.hsb[ng % 2][:, :],
                                                self.sgsb[ng % 2][:, :],
                                                self.bank(pb + 1), ALU.mult))
                    ng += 1

        @blk.sync
        def _(g):
            ng = 0
            for j in range(NJ):
                for ct in range(NCT):
                    g.wait_ge(self.VE.h, ve0 + ng + 1)
                    tk = self.sAV.issue(g.dma_start(
                        out=self.h_spill[j, :, ct * 512:(ct + 1) * 512],
                        in_=self.hsb[ng % 2][:, :]), ng)
                    assert tk == h_pred[ng]
                    ng += 1
        self.ffn1_h_tok = [h_pred[-2], h_pred[-1]]
        self.ffn1_te_last = self.TE.n

    def ph_ffn2(self):
        blk = self.block
        NCT = 10
        w3_tok = [None, None]
        hct_tok = [[None] * NCT for _ in range(2)]
        ev_pred = self.sYB.predict([nb % 4 for nb in range(2 * NCT * 4)])
        te0, ve0 = self.TE.n, self.VE.n

        @blk.gpsimd
        def _(g):
            for t in self.ffn1_h_tok:
                self.sAV.wait(g, t)
            for dh in range(2):
                if dh == 1:
                    g.wait_ge(self.TE.h, te0 + NCT * 4)
                w3_tok[dh] = self.sW.issue(g.dma_start(
                    out=self.w3h[:, :, :],
                    in_=self.w3T[:, dh * 512:(dh + 1) * 512]
                        .rearrange("(c p) d -> p c d", p=128)), dh)

        @blk.scalar
        def _(s):
            for dh in range(2):
                for ct in range(NCT):
                    gi = dh * NCT + ct
                    if gi >= 2:
                        s.wait_ge(self.TE.h, te0 + (gi - 1) * 4)
                    hct_tok[dh][ct] = self.sXL.issue(s.dma_start(
                        out=self.hct[gi % 2][:, :, :],
                        in_=self.h_spill[:, :, ct * 512:(ct + 1) * 512]
                            .rearrange("j p f -> p j f")), gi)

        @blk.tensor
        def _(t):
            for dh in range(2):
                self.sW.wait(t, w3_tok[dh])
                for ct in range(NCT):
                    gi = dh * NCT + ct
                    self.sXL.wait(t, hct_tok[dh][ct])
                    for cc in range(4):
                        nb = gi * 4 + cc
                        if nb >= 8:
                            t.wait_ge(self.VE.h, ve0 + (nb - 8) + 1)
                        b = nb % 8
                        for j in range(32):
                            ins = t.matmul(self.bank(b),
                                           self.hct[gi % 2][:, j, cc * 128:(cc + 1) * 128],
                                           self.w3h[:, j, :],
                                           start=(j == 0), stop=(j == 31))
                        self.TE.inc(ins)

        @blk.vector
        def _(v):
            nb = 0
            for dh in range(2):
                for ct in range(NCT):
                    for cc in range(4):
                        v.wait_ge(self.TE.h, te0 + nb + 1)
                        if nb >= 4:
                            self.sYB.wait(v, ev_pred[nb - 4])
                        sl = self.ybe[nb % 2][:, (nb // 2 % 2) * 512:(nb // 2 % 2) * 512 + 512]
                        self.VE.inc(v.tensor_copy(sl, self.bank(nb % 8)))
                        nb += 1

        @blk.sync
        def _(g):
            nb = 0
            for dh in range(2):
                for ct in range(NCT):
                    for cc in range(4):
                        g.wait_ge(self.VE.h, ve0 + nb + 1)
                        sl = self.ybe[nb % 2][:, (nb // 2 % 2) * 512:(nb // 2 % 2) * 512 + 512]
                        tk = self.sYB.issue(g.dma_start(
                            out=self.c_yb_in[ct * 512 + cc * 128: ct * 512 + (cc + 1) * 128,
                                             dh * 512:(dh + 1) * 512],
                            in_=sl), nb % 4)
                        assert tk == ev_pred[nb]
                        nb += 1

        @blk.gpsimd
        def _(g):
            for tk in (ev_pred[-4], ev_pred[-3], ev_pred[-2], ev_pred[-1]):
                self.sYB.wait(g, tk)
            self.GE.inc(g.collective_compute(
                "AllGather", ALU.bypass, replica_groups=[list(range(NE))],
                ins=[self.c_yb_in[:, :]], outs=[self.c_yb_out[:, :]]))
        self.yb_ag_ge = self.GE.n

    def ph_combine(self):
        blk = self.block
        rc = self.rcol
        ve0 = self.VE.n
        r_tok = [[None, None] for _ in range(NG)]
        x_tok = [None] * NG
        st_tok = self.sLNST.predict(list(range(NG)))
        VPT = 3

        @blk.gpsimd
        def _(g):
            g.wait_ge(self.GE.h, self.yb_ag_ge)
            for tk in self.slots_tok:
                self.sSLOT.wait(g, tk)
            from concourse.bass_core_rust import IndirectOffsetOnAxis
            for gi in range(NG):
                if gi >= 2:
                    g.wait_ge(self.VE.h, ve0 + (gi - 2) * VPT + 2)
                r_tok[gi][0] = self.sIG.issue(g.indirect_dma_start(
                    out=self.rows0[gi % 2][:, :], out_offset=None,
                    in_=self.c_yb_out[:, :],
                    in_offset=IndirectOffsetOnAxis(self.sl0_i[:, gi:gi + 1], 0)), gi)
                r_tok[gi][1] = self.sIG.issue(g.indirect_dma_start(
                    out=self.rows1[gi % 2][:, :], out_offset=None,
                    in_=self.c_yb_out[:, :],
                    in_offset=IndirectOffsetOnAxis(self.sl1_i[:, gi:gi + 1], 0)), gi)

        @blk.scalar
        def _(s):
            s.wait_ge(self.VE.h, ve0)
            for gi in range(NG):
                if gi >= 2:
                    s.wait_ge(self.VE.h, ve0 + (gi - 2) * VPT + VPT)
                x_tok[gi] = self.sXL.issue(s.dma_start(
                    out=self.xt[gi % 2][:, :],
                    in_=self.x_dram[gi * 128:(gi + 1) * 128, :]), gi)

        @blk.vector
        def _(v):
            for gi in range(NG):
                self.sIG.wait(v, r_tok[gi][1])
                self.sXL.wait(v, x_tok[gi])
                if gi >= 2:
                    self.sLNST.wait(v, st_tok[gi - 2])
                self.VE.inc(v.tensor_scalar(self.nrm[gi % 2][:, :], self.rows0[gi % 2][:, :],
                                            rc["gw0"][:, gi:gi + 1], None, ALU.mult))
                v.wait_ge(self.VE.h, self.VE.n)
                self.VE.inc(v.scalar_tensor_tensor(self.nrm[gi % 2][:, :],
                                                   self.rows1[gi % 2][:, :],
                                                   rc["gw1"][:, gi:gi + 1],
                                                   self.nrm[gi % 2][:, :],
                                                   ALU.mult, ALU.add))
                v.wait_ge(self.VE.h, self.VE.n)
                self.VE.inc(v.tensor_add(self.nrm[gi % 2][:, :], self.nrm[gi % 2][:, :],
                                         self.xt[gi % 2][:, :]))

        @blk.sync
        def _(g):
            for gi in range(NG):
                g.wait_ge(self.VE.h, ve0 + gi * VPT + VPT)
                dst = self.out_x if self.upto == 2 else self.x_dram
                tk = self.sLNST.issue(g.dma_start(
                    out=dst[gi * 128:(gi + 1) * 128, :], in_=self.nrm[gi % 2][:, :]), gi)
                assert tk == st_tok[gi]
        self.final_tok = [st_tok[NG - 2], st_tok[NG - 1]]

    def ph_consts_l1(self):
        nc, blk = self.nc, self.block
        dtri = np.tril(np.ones((128, 128), np.float32)).T
        smsk = np.zeros((64, 4, 512), np.float32)
        for b in range(4):
            for a in range(4):
                for gg in range(16):
                    kpos = 128 * gg + 32 * a + 31
                    smsk[a * 16 + gg, b, :] = (kpos <= (b * 512 + np.arange(512))).astype(np.float32)
        c_dtri = nc.inline_tensor(dtri.astype(np.float32), name="c_dtri")
        c_smsk = nc.inline_tensor(smsk.reshape(64, 4 * 512), name="c_smsk")
        toks = {}

        @blk.gpsimd
        def _(g):
            self.sTR.wait(g, self.xnt_tok)
            toks['dtri'] = self.sC.issue(g.dma_start(out=self.DTRI[:, :], in_=c_dtri.ap()), 0)
            toks['smsk'] = self.sC.issue(g.dma_start(
                out=self.SMSK.ap().rearrange("p b q -> p (b q)"), in_=c_smsk.ap()), 0)
            toks['bias'] = self.sC.issue(g.dma_start(
                out=self.bias_sb[:, :],
                in_=self.cab.rearrange("o (c p) -> (o p) c", p=128),
                allow_slow_non_contiguous=True), 0)
        self.l1c_tok = toks['bias']

        te0, ve0 = self.TE.n, self.VE.n
        gtok = {}

        @blk.sync
        def _(g):
            g.wait_ge(self.VE.h, ve0)
            gtok['bv'] = self.sG.issue(g.dma_start(out=self.grow[:, :],
                                                   in_=self.cab[:, 2048:3072]), 0)

        @blk.tensor
        def _(t):
            self.sG.wait(t, gtok['bv'])
            for half in range(2):
                ins = t.matmul(self.bank(half), self.ones_f[:, :],
                               self.grow[:, half * 512:(half + 1) * 512],
                               start=True, stop=True)
            self.TE.inc(ins)

        @blk.vector
        def _(v):
            v.wait_ge(self.TE.h, te0 + 1)
            self.VE.inc(v.tensor_copy(self.bvb[:, 0:512], self.bank(0)))
            v.wait_ge(self.VE.h, self.VE.n)
            self.VE.inc(v.tensor_copy(self.bvb[:, 512:1024], self.bank(1)))

        @blk.sync
        def _(g):
            g.wait_ge(self.VE.h, ve0 + 2)
            gtok['cpb'] = self.sG.issue(g.dma_start(out=self.grow[:, :], in_=self.cpb), 0)

        @blk.tensor
        def _(t):
            self.sG.wait(t, gtok['cpb'])
            for half in range(2):
                ins = t.matmul(self.bank(half), self.ones_f[:, :],
                               self.grow[:, half * 512:(half + 1) * 512],
                               start=True, stop=True)
            self.TE.inc(ins)

        @blk.vector
        def _(v):
            v.wait_ge(self.TE.h, te0 + 2)
            self.VE.inc(v.tensor_copy(self.cpbb[:, 0:512], self.bank(0)))
            v.wait_ge(self.VE.h, self.VE.n)
            self.VE.inc(v.tensor_copy(self.cpbb[:, 512:1024], self.bank(1)))
        self.l1c_ve = self.VE.n

    def ph_qkv_l1(self):
        blk = self.block
        wq, wv = self.wbuf, self.wvbuf
        te0, ve0, se0 = self.TE.n, self.VE.n, self.SE.n
        w_tok = [None] * 20

        @blk.gpsimd
        def _(g):
            for i in range(16):
                proj, ec = divmod(i, NCH)
                if i >= 2:
                    g.wait_ge(self.TE.h, te0 + (i - 1) * 4)
                w_tok[i] = self.sW.issue(g.dma_start(
                    out=wq[i % 2][:, :, 0:128],
                    in_=self.caw[:, proj * D + ec * 128: proj * D + (ec + 1) * 128]
                        .rearrange("(c p) e -> p c e", p=128)), i)
            for hf in range(4):
                g.wait_ge(self.TE.h, te0 + 64 + hf * 16 if hf else te0 + 60)
                w_tok[16 + hf] = self.sW.issue(g.dma_start(
                    out=wv[:, :, :],
                    in_=self.caw[:, 2 * D + hf * 256: 2 * D + (hf + 1) * 256]
                        .rearrange("(c p) e -> p c e", p=128)), hf)

        @blk.tensor
        def _(t):
            self.sTR.wait(t, self.xnt_tok)
            t.wait_ge(self.VE.h, self.l1c_ve)
            ng = 0
            for i in range(16):
                self.sW.wait(t, w_tok[i])
                for qc in range(4):
                    b = ng % 8
                    if ng >= 8:
                        if ng - 8 < 64:
                            t.wait_ge(self.SE.h, se0 + (ng - 8) + 1)
                        else:
                            t.wait_ge(self.VE.h, ve0 + 1 + (ng - 8 - 64) + 1)
                    for dc in range(NCH):
                        ins = t.matmul(self.bank(b), wq[i % 2][:, dc, 0:128],
                                       self.XNT[:, dc, qc * 512:(qc + 1) * 512],
                                       start=(dc == 0), stop=(dc == NCH - 1))
                    self.TE.inc(ins)
                    ng += 1
            for hf in range(4):
                self.sW.wait(t, w_tok[16 + hf])
                for tc in range(NG):
                    b = ng % 8
                    if ng >= 8:
                        if ng - 8 < 64:
                            t.wait_ge(self.SE.h, se0 + (ng - 8) + 1)
                        else:
                            t.wait_ge(self.VE.h, ve0 + 1 + (ng - 8 - 64) + 1)
                    for dc in range(NCH):
                        ins = t.matmul(self.PS[:, (b % 8) * 512:(b % 8) * 512 + 256],
                                       self.XNT[:, dc, tc * 128:(tc + 1) * 128],
                                       wv[:, dc, :],
                                       start=(dc == 0), stop=(dc == NCH - 1))
                    self.TE.inc(ins)
                    ng += 1

        @blk.scalar
        def _(s):
            self.sC.wait(s, self.l1c_tok)
            ng = 0
            for i in range(16):
                proj, ec = divmod(i, NCH)
                dstT = self.QT if proj == 0 else self.KT
                bcol = self.bias_sb[:, proj * 8 + ec: proj * 8 + ec + 1]
                for qc in range(4):
                    s.wait_ge(self.TE.h, te0 + ng + 1)
                    self.SE.inc(s.activation(out=dstT[:, ec, qc * 512:(qc + 1) * 512],
                                             in_=self.bank(ng % 8), func=AF.Identity,
                                             bias=bcol, scale=1.0))
                    ng += 1

        @blk.vector
        def _(v):
            self.VE.inc(v.memset(self.VA[:, :, 64::65], 1.0))
            ng = 64
            nv = 0
            for hf in range(4):
                for tc in range(NG):
                    v.wait_ge(self.TE.h, te0 + ng + 1)
                    dst = self.VA[:, tc, hf * 260:(hf + 1) * 260]
                    dst = dst.rearrange("p (h j) -> p h j", h=4)[:, :, 0:64]
                    src2 = self.PS[:, (ng % 8) * 512:(ng % 8) * 512 + 256] \
                        .rearrange("p (h j) -> p h j", h=4)
                    self.VE.inc(v.tensor_copy(dst, src2))
                    ng += 1
                    nv += 1
        self.qkv1_se = self.SE.n
        self.qkv1_ve = self.VE.n

        vsa_tok = self.sSL2.predict([0, 0, 0, 0])[-1]

        @blk.sync
        def _(g):
            g.wait_ge(self.VE.h, self.qkv1_ve)
            for a in range(4):
                tk = self.sSL2.issue(g.dma_start(
                    out=self.VSA[a * 16:(a + 1) * 16, :]
                        .rearrange("g (o c) -> g o c", o=1),
                    in_=self.VA[31 + 32 * a:32 + 32 * a, :, :]), 0)
            assert tk == vsa_tok
        self.vsa_tok = vsa_tok

    def ph_heads_l1(self):
        blk = self.block

        @blk.vector
        def _(v):
            v.wait_ge(self.VE.h, self.VE.n)
            for ch in range(NCH):
                self.VE.inc(v.tensor_copy(
                    self.KSTR[:, ch, :].rearrange("p (a g) -> p a g", a=4),
                    self.KT[:, ch, :].rearrange("p (g x) -> p g x", g=16)[:, :, 31::32]
                        .transpose([0, 2, 1])))

        te0, se0, ve0 = self.TE.n, self.SE.n, self.VE.n
        sdone, edone, mdone, avdone, evdone = {}, {}, {}, {}, {}
        te, se, ve = te0, se0, ve0
        for it in range(64):
            for c5 in range(5):
                te += 1
                sdone[(it, c5)] = te
                se += 1
                edone[(it, c5)] = se
                ve += (1 if c5 == 0 else (2 if c5 >= 2 else 1))
                mdone[(it, c5)] = ve
                te += 1
                avdone[(it, c5)] = te
            ve += 1
            evdone[it] = ve
        av_tok = [[None, None] for _ in range(NH)]
        flat = self.sAV.predict([p for h in range(NH) for p in (h, h)])
        avst = [[flat[2 * h], flat[2 * h + 1]] for h in range(NH)]

        @blk.tensor
        def _(t):
            self.sSL2.wait(t, self.vsa_tok)
            t.wait_ge(self.SE.h, self.qkv1_se)
            t.wait_ge(self.VE.h, ve0)
            for it in range(64):
                h, b = divmod(it, 4)
                kA = (h % 2) * 64
                ch = h // 2
                par = it % 2
                for c5 in range(5):
                    if c5 >= 2:
                        t.wait_ge(self.SE.h, edone[(it, c5 - 2)])
                    elif it >= 1:
                        t.wait_ge(self.SE.h, edone[(it - 1, 4 if c5 == 0 else 3)])
                    if c5 == 0:
                        ins = t.matmul(self.PS[0:64, (c5 % 2) * 512:(c5 % 2) * 512 + 512],
                                       self.KSTR[kA:kA + 64, ch, :],
                                       self.QT[kA:kA + 64, ch, b * 512:(b + 1) * 512],
                                       start=True, stop=True)
                    else:
                        m = c5 - 1
                        ins = t.matmul(self.PS[:, (c5 % 2) * 512:(c5 % 2) * 512 + 512],
                                       self.KT[kA:kA + 64, ch,
                                               b * 512 + m * 128: b * 512 + (m + 1) * 128],
                                       self.QT[kA:kA + 64, ch, b * 512:(b + 1) * 512],
                                       start=True, stop=True)
                    self.TE.inc(ins)
                    assert self.TE.n == sdone[(it, c5)]
                    t.wait_ge(self.VE.h, mdone[(it, c5)])
                    if c5 == 0:
                        if it >= 1:
                            t.wait_ge(self.VE.h, evdone[it - 1])
                        ins = t.matmul(self.PS[0:65, 2048:2560],
                                       self.VSA[:, h * 65:(h + 1) * 65],
                                       self.PL1[par][0:64, 0, :],
                                       start=True, stop=False)
                    else:
                        m = c5 - 1
                        ins = t.matmul(self.PS[0:65, 2048:2560],
                                       self.VA[:, 4 * b + m, h * 65:(h + 1) * 65],
                                       self.PL1[par][:, c5, :],
                                       start=False, stop=(c5 == 4))
                    self.TE.inc(ins)
                    assert self.TE.n == avdone[(it, c5)]

        @blk.scalar
        def _(s):
            for it in range(64):
                par = it % 2
                for c5 in range(5):
                    s.wait_ge(self.TE.h, sdone[(it, c5)])
                    if it >= 2:
                        s.wait_ge(self.TE.h, avdone[(it - 2, c5)])
                    if c5 == 0:
                        ins = s.activation(out=self.PL1[par][0:64, 0, :],
                                           in_=self.PS[0:64, 0:512],
                                           func=AF.Exp, scale=0.125)
                    else:
                        m = c5 - 1
                        ins = s.activation(
                            out=self.PL1[par][:, c5, m * 128:512],
                            in_=self.PS[:, (c5 % 2) * 512 + m * 128:(c5 % 2) * 512 + 512],
                            func=AF.Exp, scale=0.125)
                    self.SE.inc(ins)
                    assert self.SE.n == edone[(it, c5)]

        @blk.vector
        def _(v):
            self.sC.wait(v, self.l1c_tok)
            for it in range(64):
                h, b = divmod(it, 4)
                par = it % 2
                for c5 in range(5):
                    v.wait_ge(self.SE.h, edone[(it, c5)])
                    if c5 == 0:
                        self.VE.inc(v.tensor_mul(self.PL1[par][0:64, 0, :],
                                                 self.PL1[par][0:64, 0, :],
                                                 self.SMSK[:, b, :]))
                    else:
                        m = c5 - 1
                        if m >= 1:
                            self.VE.inc(v.memset(self.PL1[par][:, c5, 0:m * 128], 0.0))
                            v.wait_ge(self.VE.h, self.VE.n)
                        self.VE.inc(v.tensor_mul(
                            self.PL1[par][:, c5, m * 128:(m + 1) * 128],
                            self.PL1[par][:, c5, m * 128:(m + 1) * 128],
                            self.DTRI[:, :]))
                    assert self.VE.n == mdone[(it, c5)]
                v.wait_ge(self.TE.h, avdone[(it, 4)])
                if it >= 2 and b == 0:
                    self.sAV.wait(v, avst[h - 1][1] if h >= 1 else None) if False else None
                if b == 0 and h >= 2:
                    self.sAV.wait(v, avst[h - 2][1])
                self.VE.inc(v.tensor_copy(self.av_t[h % 2][0:65, b * 512:(b + 1) * 512],
                                          self.PS[0:65, 2048:2560]))
                assert self.VE.n == evdone[it]

        @blk.sync
        def _(g):
            for h in range(NH):
                g.wait_ge(self.VE.h, evdone[h * 4 + 3])
                t0 = self.sAV.issue(
                    g.dma_start(out=self.stg_fm[h * 64:(h + 1) * 64, :],
                                in_=self.av_t[h % 2][0:64, :]), h)
                t1 = self.sAV.issue(
                    g.dma_start(out=self.stg_fm[1024 + h:1025 + h, :],
                                in_=self.av_t[h % 2][64:65, :]), h)
                assert [t0, t1] == avst[h]
        self.heads_fm_tok = [avst[NH - 2][1], avst[NH - 1][1]]

    def ph_finish_l1(self):
        blk = self.block
        ve0 = self.VE.n
        tr_tok = [None] * NG
        st_tok = self.sLNST.predict(list(range(NG)))
        VPT = 4

        @blk.sync
        def _(g):
            for t in self.heads_fm_tok:
                self.sAV.wait(g, t)
            for gi in range(NG):
                if gi >= 2:
                    g.wait_ge(self.VE.h, ve0 + (gi - 2) * VPT + 2)
                tr_tok[gi] = self.sOT.issue(
                    g.dma_start_transpose(self.ot[gi % 2][:, :],
                                          self.stg_fm[:, gi * 128:(gi + 1) * 128]), gi)
                if gi >= 1:
                    g.wait_ge(self.VE.h, ve0 + (gi - 1) * VPT + VPT)
                    tk = self.sLNST.issue(
                        g.dma_start(out=self.stg_tm[(gi - 1) * 128:gi * 128, :],
                                    in_=self.xnt_tile[(gi - 1) % 2][:, :]), gi - 1)
                    assert tk == st_tok[gi - 1]
            g.wait_ge(self.VE.h, ve0 + (NG - 1) * VPT + VPT)
            tk = self.sLNST.issue(
                g.dma_start(out=self.stg_tm[(NG - 1) * 128:NG * 128, :],
                            in_=self.xnt_tile[(NG - 1) % 2][:, :]), NG - 1)
            assert tk == st_tok[NG - 1]

        @blk.vector
        def _(v):
            for gi in range(NG):
                o = self.ot[gi % 2]
                self.sOT.wait(v, tr_tok[gi])
                if gi >= 2:
                    self.sLNST.wait(v, st_tok[gi - 2])
                self.VE.inc(v.reciprocal(self.zinv[gi % 2][:, :], o[:, 1024:1040]))
                v.wait_ge(self.VE.h, self.VE.n)
                zb = self.zinv[gi % 2].ap().rearrange("p (h o) -> p h o", o=1) \
                    .broadcast_to([128, 16, 64])
                self.VE.inc(v.tensor_tensor(
                    self.nrm[gi % 2].ap().rearrange("p (h j) -> p h j", h=16),
                    o[:, 0:1024].rearrange("p (h j) -> p h j", h=16),
                    zb, ALU.mult))
                v.wait_ge(self.VE.h, self.VE.n)
                self.VE.inc(v.tensor_add(self.nrm[gi % 2][:, :], self.nrm[gi % 2][:, :],
                                         self.bvb[:, :]))
                v.wait_ge(self.VE.h, self.VE.n)
                self.VE.inc(v.tensor_copy(self.xnt_tile[gi % 2][:, :], self.nrm[gi % 2][:, :]))
        self.ln_store_tok = [st_tok[NG - 2], st_tok[NG - 1]]
        self.ph_transpose_tm_to_fm(self.stg_tm, self.XNT, NCH)

        te1, ve1 = self.TE.n, self.VE.n
        cpw_tok = self.sW.predict([0])[0]
        x_tok = [None] * NG
        ost_tok = self.sOS.predict([gi % 2 for gi in range(NG)])

        @blk.gpsimd
        def _(g):
            g.wait_ge(self.VE.h, ve1)
            tk = self.sW.issue(g.dma_start(
                out=self.cpw_sb[:, :, :],
                in_=self.cpw.rearrange("(c p) e -> p c e", p=128)), 0)
            assert tk == cpw_tok

        @blk.tensor
        def _(t):
            self.sTR.wait(t, self.xnt_tok)
            self.sW.wait(t, cpw_tok)
            for tc in range(NG):
                for half in range(2):
                    b = (tc * 2 + half) % 8
                    if tc >= 4:
                        t.wait_ge(self.VE.h, ve1 + (tc - 4) * 3 + 2)
                    for dc in range(NCH):
                        ins = t.matmul(self.PS[:, b * 512:(b + 1) * 512],
                                       self.XNT[:, dc, tc * 128:(tc + 1) * 128],
                                       self.cpw_sb[:, dc, half * 512:(half + 1) * 512],
                                       start=(dc == 0), stop=(dc == NCH - 1))
                    self.TE.inc(ins)

        @blk.scalar
        def _(s):
            s.wait_ge(self.VE.h, ve1)
            for gi in range(NG):
                if gi >= 2:
                    s.wait_ge(self.VE.h, ve1 + (gi - 2) * 3 + 3)
                x_tok[gi] = self.sXL.issue(s.dma_start(
                    out=self.xt[gi % 2][:, :],
                    in_=self.x_dram[gi * 128:(gi + 1) * 128, :]), gi)

        @blk.vector
        def _(v):
            for tc in range(NG):
                for half in range(2):
                    ng = tc * 2 + half
                    v.wait_ge(self.TE.h, te1 + ng + 1)
                    if tc >= 2 and half == 0:
                        self.sOS.wait(v, ost_tok[tc - 2])
                    self.VE.inc(v.tensor_add(
                        self.nrm[tc % 2][:, half * 512:(half + 1) * 512],
                        self.PS[:, (ng % 8) * 512:(ng % 8) * 512 + 512],
                        self.cpbb[:, half * 512:(half + 1) * 512]))
                self.sXL.wait(v, x_tok[tc])
                v.wait_ge(self.VE.h, self.VE.n)
                self.VE.inc(v.tensor_add(self.nrm[tc % 2][:, :], self.nrm[tc % 2][:, :],
                                         self.xt[tc % 2][:, :]))

        @blk.sync
        def _(g):
            for gi in range(NG):
                g.wait_ge(self.VE.h, ve1 + gi * 3 + 3)
                dst = self.out_x if self.upto == 3 else self.x_dram
                tk = self.sOS.issue(g.dma_start(
                    out=dst[gi * 128:(gi + 1) * 128, :], in_=self.nrm[gi % 2][:, :]), gi)
                assert tk == ost_tok[gi]
        self.final_tok = [ost_tok[NG - 2], ost_tok[NG - 1]]
        self.x_store_tok = ost_tok

    def ph_l1ffn_route(self):
        blk = self.block
        sm = self.small
        r8 = self.r8
        nc = self.nc
        self.l1f_te_entry = self.TE.n
        c_i128 = nc.inline_tensor(np.eye(128, dtype=np.float32), name="c_i128b")
        gtok = {}

        @blk.gpsimd
        def _(g):
            g.wait_ge(self.TE.h, self.l1f_te_entry)
            gtok['gw'] = self.sC.issue(g.dma_start(
                out=self.rw_sb[:, :, :],
                in_=self.gwT.rearrange("(c p) e -> p c e", p=128)), 0)
            gtok['i128b'] = self.sC.issue(g.dma_start(
                out=self.I128b[:, :], in_=c_i128.ap()), 0)

        te0, ve0, se0 = self.TE.n, self.VE.n, self.SE.n

        @blk.tensor
        def _(t):
            self.sTR.wait(t, self.xnt_tok)
            self.sC.wait(t, gtok['i128b'])
            t.wait_ge(self.VE.h, ve0)
            for tc in range(NG):
                if tc >= 4:
                    t.wait_ge(self.SE.h, se0 + tc - 3)
                for dc in range(NCH):
                    ins = t.matmul(self.PS[:, (tc % 4) * 512:(tc % 4) * 512 + 8],
                                   self.XNT[:, dc, tc * 128:(tc + 1) * 128],
                                   self.rw_sb[:, dc, :],
                                   start=(dc == 0), stop=(dc == NCH - 1))
                self.TE.inc(ins)

        @blk.scalar
        def _(s):
            s.wait_ge(self.VE.h, ve0)
            for tc in range(NG):
                s.wait_ge(self.TE.h, te0 + tc + 1)
                self.SE.inc(s.activation(
                    out=self.logits[:, tc, :],
                    in_=self.PS[:, (tc % 4) * 512:(tc % 4) * 512 + 8],
                    func=AF.Sigmoid))

        ve1 = self.VE.n
        VPT = 14

        @blk.vector
        def _(v):
            for tc in range(NG):
                sc = self.logits[:, tc, :]

                def C(ins):
                    self.VE.inc(ins)
                    v.wait_ge(self.VE.h, self.VE.n)
                v.wait_ge(self.SE.h, se0 + tc + 1)
                C(v.reduce_max(sm[:, 8:9], sc, axis=AX.X))
                C(v.tensor_scalar(r8["oh1"][:, :], sc, sm[:, 8:9], None, ALU.is_ge))
                C(v.scalar_tensor_tensor(r8["xm"][:, :], r8["oh1"][:, :], -2.0e30, sc,
                                         ALU.mult, ALU.add))
                C(v.reduce_max(sm[:, 9:10], r8["xm"][:, :], axis=AX.X))
                C(v.tensor_scalar(r8["oh2"][:, :], r8["xm"][:, :], sm[:, 9:10], None,
                                  ALU.is_ge))
                C(v.tensor_add(sm[:, 10:11], sm[:, 8:9], sm[:, 9:10]))
                C(v.tensor_scalar(sm[:, 10:11], sm[:, 10:11], 1e-20, None, ALU.add))
                C(v.reciprocal(sm[:, 11:12], sm[:, 10:11]))
                C(v.tensor_scalar(sm[:, 12:13], sm[:, 11:12], 2.5, None, ALU.mult))
                C(v.tensor_mul(sm[:, 13:14], sm[:, 8:9], sm[:, 12:13]))
                C(v.tensor_mul(sm[:, 14:15], sm[:, 9:10], sm[:, 12:13]))
                C(v.tensor_scalar(r8["oh1"][:, :], r8["oh1"][:, :], sm[:, 13:14], None,
                                  ALU.mult))
                C(v.tensor_scalar(r8["oh2"][:, :], r8["oh2"][:, :], sm[:, 14:15], None,
                                  ALU.mult))
                C(v.tensor_add(self.base[:, tc, :], r8["oh1"][:, :], r8["oh2"][:, :]))
                assert self.VE.n == ve1 + (tc + 1) * VPT
        self.l1r_i128_tok = gtok['i128b']

    def ph_l1ffn_main(self):
        blk = self.block
        NSL = 10
        te0, ve0, se0 = self.TE.n, self.VE.n, self.SE.n
        gu_te, tr_te, dn_te = {}, {}, {}
        silu_se, h_ve, evT_ve, acc_ve, fin_ve = {}, {}, {}, {}, {}
        gu_order, tr_order, dn_order = [], [], []
        te = te0
        for ps in range(2):
            for sl in range(NSL):
                seq = [('gu', 0), ('gu', 1)]
                for q in range(2, 8):
                    seq += [('tr', q - 2), ('dn', q - 2), ('gu', q)]
                seq += [('tr', 6), ('dn', 6), ('tr', 7), ('dn', 7)]
                for kind, q in seq:
                    te += 1
                    {'gu': gu_te, 'tr': tr_te, 'dn': dn_te}[kind][(ps, sl, q)] = te
                    {'gu': gu_order, 'tr': tr_order,
                     'dn': dn_order}[kind].append((ps, sl, q))
        lin = [(ps, sl, q) for ps in range(2) for sl in range(NSL) for q in range(8)]
        se = se0
        ve = ve0
        for ps, sl, q in lin:
            se += 1
            silu_se[(ps, sl, q)] = se
            ve += 1
            h_ve[(ps, sl, q)] = ve
            ve += 1
            evT_ve[(ps, sl, q)] = ve
            ve += 1
            acc_ve[(ps, sl, q)] = ve
            if sl == NSL - 1:
                ve += 1
                fin_ve[(ps, q)] = ve
        gu_pos = {k: i for i, k in enumerate(gu_order)}
        tr_pos = {k: i for i, k in enumerate(tr_order)}
        dn_pos = {k: i for i, k in enumerate(dn_order)}
        wgu_tok, wd_tok = {}, {}
        xsrc_tok = self.x_store_tok
        x_tok = self.sXL.predict([g % 2 for g in range(16)])
        ost_tok = self.sOS.predict([g % 2 for g in range(16)])

        @blk.gpsimd
        def _(g):
            for ps in range(2):
                for sl in range(NSL):
                    gsl = ps * NSL + sl
                    if gsl >= 2:
                        pk = (gsl - 2) // NSL, (gsl - 2) % NSL
                        g.wait_ge(self.TE.h, gu_te[(pk[0], pk[1], 7)])
                    else:
                        g.wait_ge(self.TE.h, self.l1f_te_entry)
                    if sl < 8:
                        srcg, srcu = self.wgT[sl, :, :], self.wuT[sl, :, :]
                    else:
                        srcg = self.wsgT[:, (sl - 8) * 512:(sl - 7) * 512]
                        srcu = self.wsuT[:, (sl - 8) * 512:(sl - 7) * 512]
                    self.sW.issue(g.dma_start(
                        out=self.wgu[sl % 2][:, :, 0:512],
                        in_=srcg.rearrange("(c p) f -> p c f", p=128)), gsl)
                    wgu_tok[gsl] = self.sW.issue(g.dma_start(
                        out=self.wgu[sl % 2][:, :, 512:1024],
                        in_=srcu.rearrange("(c p) f -> p c f", p=128)), gsl)
                    if gsl >= 2:
                        pk = (gsl - 2) // NSL, (gsl - 2) % NSL
                        g.wait_ge(self.TE.h, dn_te[(pk[0], pk[1], 7)])
                    if sl < 8:
                        srcd = self.wdT[sl, :, :]
                    else:
                        srcd = self.wsdT[(sl - 8) * 512:(sl - 7) * 512, :]
                    wd_tok[gsl] = self.sWD.issue(g.dma_start(
                        out=self.wd[sl % 2][:, :, :],
                        in_=srcd.rearrange("(ic p) h -> p ic h", p=128)), gsl)

        @blk.tensor
        def _(t):
            t.wait_ge(self.VE.h, ve0)
            t.wait_ge(self.SE.h, se0)
            first_tr = [True]
            for ps in range(2):
                for sl in range(NSL):
                    gsl = ps * NSL + sl
                    seq = [('gu', 0), ('gu', 1)]
                    for q in range(2, 8):
                        seq += [('tr', q - 2), ('dn', q - 2), ('gu', q)]
                    seq += [('tr', 6), ('dn', 6), ('tr', 7), ('dn', 7)]
                    for kind, q in seq:
                        g16 = ps * 8 + q
                        key = (ps, sl, q)
                        if kind == 'gu':
                            if q == 0:
                                self.sW.wait(t, wgu_tok[gsl])
                            i = gu_pos[key]
                            if i >= 2:
                                pk = gu_order[i - 2]
                                t.wait_ge(self.SE.h, silu_se[pk])
                                t.wait_ge(self.VE.h, h_ve[pk])
                            bg = (q % 2) * 2
                            for dc in range(NCH):
                                t.matmul(self.bank(bg),
                                         self.XNT[:, dc, g16 * 128:(g16 + 1) * 128],
                                         self.wgu[sl % 2][:, dc, 0:512],
                                         start=(dc == 0), stop=(dc == NCH - 1))
                            for dc in range(NCH):
                                ins = t.matmul(self.bank(bg + 1),
                                               self.XNT[:, dc, g16 * 128:(g16 + 1) * 128],
                                               self.wgu[sl % 2][:, dc, 512:1024],
                                               start=(dc == 0), stop=(dc == NCH - 1))
                            assert self.TE.inc(ins) == gu_te[key]
                        elif kind == 'tr':
                            if first_tr[0]:
                                self.sC.wait(t, self.l1r_i128_tok)
                                first_tr[0] = False
                            t.wait_ge(self.VE.h, h_ve[key])
                            i = tr_pos[key]
                            if i >= 2:
                                t.wait_ge(self.VE.h, evT_ve[tr_order[i - 2]])
                            c0 = (4 + q % 2) * 512
                            for j in range(4):
                                ins = t.matmul(
                                    self.PS[:, c0 + j * 128:c0 + (j + 1) * 128],
                                    self.hsb[q % 2][:, j * 128:(j + 1) * 128],
                                    self.I128b[:, :], start=True, stop=True)
                            assert self.TE.inc(ins) == tr_te[key]
                        else:
                            if q == 0:
                                self.sWD.wait(t, wd_tok[gsl])
                            t.wait_ge(self.VE.h, evT_ve[key])
                            i = dn_pos[key]
                            if i >= 1:
                                t.wait_ge(self.VE.h, acc_ve[dn_order[i - 1]])
                            for half in range(2):
                                for ic in range(4):
                                    ins = t.matmul(
                                        self.PS[:, 3072 + half * 512:3584 + half * 512],
                                        self.hT[q % 2][:, ic, :],
                                        self.wd[sl % 2][:, ic, half * 512:(half + 1) * 512],
                                        start=(ic == 0), stop=(ic == 3))
                            assert self.TE.inc(ins) == dn_te[key]

        @blk.scalar
        def _(s):
            s.wait_ge(self.VE.h, ve0)
            for i, (ps, sl, q) in enumerate(lin):
                s.wait_ge(self.TE.h, gu_te[(ps, sl, q)])
                if i >= 2:
                    s.wait_ge(self.VE.h, h_ve[lin[i - 2]])
                ins = s.activation(out=self.sgsb[q % 2][:, :],
                                   in_=self.bank((q % 2) * 2), func=AF.Silu)
                assert self.SE.inc(ins) == silu_se[(ps, sl, q)]

        @blk.vector
        def _(v):
            for i, (ps, sl, q) in enumerate(lin):
                g16 = ps * 8 + q
                key = (ps, sl, q)
                v.wait_ge(self.SE.h, silu_se[key])
                if i >= 2:
                    v.wait_ge(self.TE.h, tr_te[lin[i - 2]])
                if sl < 8:
                    ins = v.scalar_tensor_tensor(
                        self.hsb[q % 2][:, :], self.sgsb[q % 2][:, :],
                        self.base[:, g16, sl:sl + 1],
                        self.bank((q % 2) * 2 + 1), ALU.mult, ALU.mult)
                else:
                    ins = v.tensor_tensor(self.hsb[q % 2][:, :], self.sgsb[q % 2][:, :],
                                          self.bank((q % 2) * 2 + 1), ALU.mult)
                assert self.VE.inc(ins) == h_ve[key]
                v.wait_ge(self.TE.h, tr_te[key])
                if i >= 2:
                    v.wait_ge(self.TE.h, dn_te[lin[i - 2]])
                ins = v.tensor_copy(self.hT[q % 2].ap().rearrange("p m a -> p (m a)"),
                                    self.PS[:, (4 + q % 2) * 512:(5 + q % 2) * 512])
                assert self.VE.inc(ins) == evT_ve[key]
                v.wait_ge(self.TE.h, dn_te[key])
                if sl == 0:
                    ins = v.tensor_copy(self.acc[:, q, :], self.PS[:, 3072:4096])
                else:
                    v.wait_ge(self.VE.h, acc_ve[(ps, sl - 1, q)])
                    ins = v.tensor_tensor(self.acc[:, q, :], self.acc[:, q, :],
                                          self.PS[:, 3072:4096], ALU.add)
                assert self.VE.inc(ins) == acc_ve[key]
                if sl == NSL - 1:
                    v.wait_ge(self.VE.h, self.VE.n)
                    self.sXL.wait(v, x_tok[g16])
                    if g16 >= 2:
                        self.sOS.wait(v, ost_tok[g16 - 2])
                    ins = v.tensor_add(self.nrm[q % 2][:, :], self.acc[:, q, :],
                                       self.xt[q % 2][:, :])
                    assert self.VE.inc(ins) == fin_ve[(ps, q)]

        @blk.sync
        def _(g):
            def emit_load(g16):
                if g16 >= 2:
                    g.wait_ge(self.VE.h, fin_ve[(g16 - 2) // 8, (g16 - 2) % 8])
                else:
                    g.wait_ge(self.VE.h, ve0)
                self.sOS.wait(g, xsrc_tok[g16])
                tk = self.sXL.issue(g.dma_start(
                    out=self.xt[g16 % 2][:, :],
                    in_=self.x_dram[g16 * 128:(g16 + 1) * 128, :]), g16)
                assert tk == x_tok[g16]

            def emit_store(g16):
                g.wait_ge(self.VE.h, fin_ve[g16 // 8, g16 % 8])
                tk = self.sOS.issue(g.dma_start(
                    out=self.out_x[g16 * 128:(g16 + 1) * 128, :],
                    in_=self.nrm[g16 % 2][:, :]), g16)
                assert tk == ost_tok[g16]
            for g16 in range(16):
                emit_load(g16)
                if g16 >= 3:
                    emit_store(g16 - 3)
            for g16 in range(13, 16):
                emit_store(g16)
        self.final_tok = [ost_tok[14], ost_tok[15]]

    def drain_all(self):
        blk = self.block

        @blk.gpsimd
        def _(g):
            g.wait_ge(self.GE.h, self.GE.n)
            g.wait_ge(self.VE.h, self.VE.n)
            g.wait_ge(self.SE.h, self.SE.n)
            g.wait_ge(self.TE.h, self.TE.n)

    def finish(self):
        blk = self.block

        @blk.sync
        def _(g):
            for t in self.final_tok:
                self.sOS.wait(g, t)

    def dump_dbg(self, sbuf_ap, n_parts, width):
        blk = self.block
        GD = Stream(self.nc, "dbg", 1)

        @blk.gpsimd
        def _(g):
            g.wait_ge(self.VE.h, self.VE.n)
            g.wait_ge(self.SE.h, self.SE.n)
            g.wait_ge(self.TE.h, self.TE.n)
            tok = GD.issue(g.dma_start(out=self.out_x[0:n_parts, 0:width], in_=sbuf_ap), 0)
            GD.wait(g, tok)

    def build(self, stage=99):
        nc = self.nc
        with nc.Block() as blk:
            self.block = blk
            self.ph_ln(self.x_in, self.stg_tm)
            if stage >= 2:
                self.ph_transpose_tm_to_fm(self.stg_tm, self.XNT, NCH)
            if stage >= 3:
                self.ph_qkv_l0()
            if stage >= 4:
                self.ph_heads_l0()
            if stage >= 5:
                self.ph_finish_attn_l0()
            if self.upto >= 2 and stage >= 6:
                self.ph_consts_l0ffn()
                self.ph_ln2()
            if self.upto >= 2 and stage >= 7:
                self.ph_route()
            if self.upto >= 2 and stage >= 8:
                self.ph_expert_in()
            if self.upto >= 2 and stage >= 9:
                self.ph_ffn1()
            if self.upto >= 2 and stage >= 10:
                self.ph_ffn2()
            if self.upto >= 2 and stage >= 11:
                self.ph_combine()
            if self.upto >= 3 and stage >= 12:
                self.ph_ln(self.x_dram, self.stg_tm)
                self.ph_transpose_tm_to_fm(self.stg_tm, self.XNT, NCH)
                self.ph_consts_l1()
                self.ph_qkv_l1()
                self.ph_heads_l1()
                self.ph_finish_l1()
            if self.upto >= 4 and stage >= 13:
                self.ph_ln(self.x_dram, self.stg_tm,
                           src_toks=self.x_store_tok, src_stream=self.sOS)
                self.ph_transpose_tm_to_fm(self.stg_tm, self.XNT, NCH)
                self.ph_l1ffn_route()
            if self.upto >= 4 and stage >= 14:
                self.ph_l1ffn_main()
            if stage >= 5:
                if (self.upto >= 2 and stage < 11) or (self.upto >= 3 and stage < 12) \
                        or (self.upto >= 4 and stage < 14):
                    self.drain_all()
                self.finish()
            if stage == 1:
                self.dump_dbg(self.xnt_tile[(NG - 1) % 2][:, :], 128, D)
            elif stage == 2:
                self.dump_dbg(self.XNT[:, 0, 0:D], 128, D)
            elif stage == 3:
                self.dump_dbg(self.QT[:, 0, 0:D], 128, D)
            elif stage == 4:
                self.dump_dbg(self.av_t[(NH - 1) % 2][:, 0:D], 128, D)
        nc.finalize()
        return nc



def _make_in_maps(inputs):
    X = np.ascontiguousarray(inputs["L0_attn_x"]).astype(np.float32)
    W = np.ascontiguousarray(inputs["L0_attn_kernel"]).astype(np.float32)
    maps = []
    for c in range(8):
        m = {"x": np.ascontiguousarray(X[c]), "wqkv": W}
        m["lngamma"] = np.ascontiguousarray(
            inputs["L0_ffn_ln_gamma"].astype(np.float32).reshape(1, -1))
        m["lnbeta"] = np.ascontiguousarray(
            inputs["L0_ffn_ln_beta"].astype(np.float32).reshape(1, -1))
        m["rwT"] = np.ascontiguousarray(inputs["L0_ffn_router_w"].astype(np.float32).T)
        m["w12T"] = np.ascontiguousarray(inputs["L0_ffn_w12"][c].astype(np.float32).T)
        m["w3T"] = np.ascontiguousarray(inputs["L0_ffn_w3"][c].astype(np.float32).T)
        m["gidp1"] = np.ascontiguousarray(
            (np.arange(2048).reshape(16, 128).T + 2048 * c + 1).astype(np.float32))
        sel = np.zeros((64, 8), np.float32)
        for r in range(8):
            if r < c:
                for e in range(8):
                    sel[r * 8 + e, e] = 1.0
        m["selm"] = sel
        m["vrows"] = (np.arange(8, dtype=np.int32) * 8 + c).reshape(8, 1)
        selr = np.zeros((64, 8), np.float32)
        for r2 in range(8):
            for r in range(8):
                if r2 < r:
                    selr[r2 * 8 + c, r] = 1.0
        m["selr"] = selr
        m["caw"] = np.ascontiguousarray(inputs["L1_attn_c_attn_w"].astype(np.float32))
        m["cab"] = np.ascontiguousarray(
            inputs["L1_attn_c_attn_b"].astype(np.float32).reshape(1, -1))
        m["cpw"] = np.ascontiguousarray(inputs["L1_attn_c_proj_w"].astype(np.float32))
        m["cpb"] = np.ascontiguousarray(
            inputs["L1_attn_c_proj_b"].astype(np.float32).reshape(1, -1))
        m["gwT"] = np.ascontiguousarray(inputs["L1_ffn_gate_weight"].astype(np.float32).T)
        m["wgT"] = np.ascontiguousarray(
            inputs["L1_ffn_w_gate"].astype(np.float32).transpose(0, 2, 1))
        m["wuT"] = np.ascontiguousarray(
            inputs["L1_ffn_w_up"].astype(np.float32).transpose(0, 2, 1))
        m["wdT"] = np.ascontiguousarray(
            inputs["L1_ffn_w_down"].astype(np.float32).transpose(0, 2, 1))
        m["wsgT"] = np.ascontiguousarray(inputs["L1_ffn_ws_gate"].astype(np.float32).T)
        m["wsuT"] = np.ascontiguousarray(inputs["L1_ffn_ws_up"].astype(np.float32).T)
        m["wsdT"] = np.ascontiguousarray(inputs["L1_ffn_ws_down"].astype(np.float32).T)
        maps.append(m)
    return maps


_CACHED = {}


def kernel(**inputs):
    from concourse.bass_utils import run_bass_kernel_spmd
    if "nc" not in _CACHED:
        m = MK(upto=4)
        _CACHED["nc"] = m.build(stage=99)
    maps = _make_in_maps(inputs)
    res = run_bass_kernel_spmd(_CACHED["nc"], maps, core_ids=list(range(8)))
    out = np.stack([np.asarray(r["out_x"]) for r in res.results], axis=0)
    return out.astype(np.float32, copy=False)
